# version 01 (final)
# baseline (speedup 1.0000x reference)
import numpy as np
import ml_dtypes
import concourse.bass as bass
import concourse.mybir as mybir
from concourse.bass_utils import run_bass_kernel_spmd

F32 = mybir.dt.float32
BF16 = mybir.dt.bfloat16
I32 = mybir.dt.int32
AF = mybir.ActivationFunctionType
ALU = mybir.AluOpType
AX = mybir.AxisListType

N_CORES = 8
B, T, D = 8, 2048, 1024
DC = D // 128
TB = T // 512
NEG = -1.0e30

ENGINES = ("tensor", "vector", "scalar", "gpsimd", "sync")


class Sem:
    __slots__ = ("h", "count", "name")

    def __init__(self, h, name):
        self.h = h
        self.name = name
        self.count = 0


class Prog:

    def __init__(self, name="prog"):
        self.nc = bass.Bass("TRN2", target_bir_lowering=False)
        self.streams = {e: [] for e in ENGINES}
        self.sems = {}
        self.bar = self.sem("bar")
        self.bar_target = 0
        self.eng_bar = {e: 0 for e in ENGINES}
        self.active = {e: False for e in ENGINES}
        self.qsem = {e: self.sem("q_" + e) for e in ("sync", "gpsimd", "scalar", "vector")}
        self.qpend = {e: 0 for e in ("sync", "gpsimd", "scalar", "vector")}
        self.last_emit = {e: None for e in ENGINES}
        self._uid = 0
        self._sbuf_bytes = 0
        self.dram = {}

    def uid(self):
        self._uid += 1
        return self._uid

    def sem(self, name):
        if name not in self.sems:
            self.sems[name] = Sem(self.nc.alloc_semaphore(name), name)
        return self.sems[name]

    def sbuf(self, name, shape, dt):
        esz = {F32: 4, BF16: 2, I32: 4}[dt]
        self._sbuf_bytes += shape[1] * esz
        return self.nc.alloc_sbuf_tensor(name, list(shape), dt)

    def dram_t(self, name, shape, dt, kind="Internal"):
        t = self.nc.dram_tensor(name, list(shape), dt, kind=kind)
        self.dram[name] = t
        return t

    def emit(self, eng, fn, inc=None, raw=False):
        if self.eng_bar[eng] < self.bar_target:
            tgt = self.bar_target
            bh = self.bar.h
            self.streams[eng].append((lambda e, t=tgt, b=bh: e.wait_ge(b, t), None))
            self.eng_bar[eng] = tgt
        incs = []
        if inc:
            for s, k in inc:
                s.count += k
                incs.append((s.h, k))
        self.streams[eng].append((fn, incs))
        self.last_emit[eng] = incs
        self.active[eng] = True

    def wait(self, eng, sem, val):
        self.streams[eng].append((lambda e, s=sem.h, v=val: e.wait_ge(s, v), None))

    def dma(self, queue, out_ap, in_ap, inc_extra=None, **kw):
        s = self.qsem[queue]
        inc = [(s, 16)]
        if inc_extra:
            inc += inc_extra
        self.emit(queue, lambda e, o=out_ap, i=in_ap, k=dict(kw): e.dma_start(out=o, in_=i, **k), inc=inc)
        self.qpend[queue] = s.count

    def barrier(self):
        for q in self.qsem:
            if self.qpend[q]:
                s = self.qsem[q]
                self.wait(q, s, s.count)
                self.qpend[q] = 0
                self.emit(q, lambda h: h.nop(), inc=[(self.bar, 1)])
                self.active[q] = False
        for e in ENGINES:
            if self.active[e]:
                le = self.last_emit[e]
                if le is not None and len(le) == 0:
                    self.bar.count += 1
                    le.append((self.bar.h, 1))
                else:
                    h, k = le[-1]
                    sem_obj = next(s for s in self.sems.values() if s.h is h)
                    self.wait(e, sem_obj, sem_obj.count)
                    self.emit(e, lambda x: x.nop(), inc=[(self.bar, 1)])
                self.active[e] = False
        self.bar_target = self.bar.count

    def finalize(self):
        nc = self.nc
        with nc.Block() as blk:
            for e in ENGINES:
                ops = self.streams[e]
                if not ops:
                    continue

                def runner(engine, ops=ops):
                    for fn, incs in ops:
                        inst = fn(engine)
                        if incs:
                            assert inst is not None, "emitted fn with incs returned None"
                            for h, k in incs:
                                inst.then_inc(h, k)

                getattr(blk, e)(runner)
        return nc


class Arena:

    def __init__(self, P, nbytes, name="arena"):
        self.t = P.nc.alloc_sbuf_tensor(name, [128, nbytes], mybir.dt.uint8)
        self.free = [(0, nbytes)]
        self.live = {}

    def alloc(self, cols, dt, tag=""):
        esz = {F32: 4, BF16: 2, I32: 4}[dt]
        size = ((cols * esz + 63) // 64) * 64
        for i, (off, sz) in enumerate(self.free):
            if sz >= size:
                if sz == size:
                    self.free.pop(i)
                else:
                    self.free[i] = (off + size, sz - size)
                ap = self.t[:, off:off + cols * esz].bitcast(dt)
                self.live[off] = (off, size)
                return ap
        raise MemoryError(f"arena OOM for {tag}: need {size}B, free={self.free}")

    def free_(self, ap):
        esz = {F32: 4, BF16: 2, I32: 4}[ap.dtype]
        off, size = self.live.pop(ap.offset * esz)
        self.free.append((off, size))
        self.free.sort()
        out = []
        for o, s in self.free:
            if out and out[-1][0] + out[-1][1] == o:
                out[-1] = (out[-1][0], out[-1][1] + s)
            else:
                out.append((o, s))
        self.free = [tuple(x) for x in out]



def ld_weight(P, dst, hbm_ap, queue="gpsimd"):
    P.dma(queue, dst, hbm_ap)


def mm(P, out_ap, lhsT_ap, rhs_ap, start, stop, mmsem=None):
    P.emit("tensor",
           lambda e, o=out_ap, l=lhsT_ap, r=rhs_ap, st=start, sp=stop: e.matmul(o, l, r, start=st, stop=sp),
           inc=[(mmsem, 1)] if mmsem else None)


def evac(P, eng, out_ap, in_ap, fn=None, inc=None):
    if fn is None:
        P.emit(eng, lambda e, o=out_ap, i=in_ap: e.tensor_copy(o, i), inc=inc)
    else:
        P.emit(eng, lambda e, o=out_ap, i=in_ap, f=fn: f(e, o, i), inc=inc)


class MatmulCtx:

    def __init__(self, P, nbanks=4, name="mmx"):
        self.P = P
        self.banks = [P.nc.alloc_psum_tensor(f"{name}_pb{i}", [128, 512], F32) for i in range(nbanks)]
        self.mms = P.sem(name + "_mm")
        self.evs = {e: P.sem(name + "_ev_" + e) for e in ("vector", "scalar", "gpsimd")}
        self.nb = nbanks
        self.rr = 0
        self.bank_ev = [None] * nbanks
        self.bank_mm = [0] * nbanks

    def wait_evacs(self, eng):
        for s in self.evs.values():
            if s.count:
                self.P.wait(eng, s, s.count)

    def tile(self, kparts, evac_eng, evac_fn, lhsT_rhs_list, out_rows=128, out_cols=512, col_off=0):
        P = self.P
        b = self.rr % self.nb
        self.rr += 1
        pb = self.banks[b]
        pap = pb[0:out_rows, col_off:col_off + out_cols]
        if self.bank_ev[b] is not None:
            be, bc = self.bank_ev[b]
            P.wait("tensor", self.evs[be], bc)
        n = len(lhsT_rhs_list)
        for i, (l, r) in enumerate(lhsT_rhs_list):
            mm(P, pap, l, r, start=(i == 0), stop=(i == n - 1),
               mmsem=self.mms if i == n - 1 else None)
        self.bank_mm[b] = self.mms.count
        P.wait(evac_eng, self.mms, self.mms.count)
        es = self.evs[evac_eng]
        P.emit(evac_eng, lambda e, a=pap, f=evac_fn: f(e, a), inc=[(es, 1)])
        self.bank_ev[b] = (evac_eng, es.count)

    def drain(self):
        pass


def t_copy(P, eng, out, in_, inc=None):
    if eng == "scalar":
        P.emit(eng, lambda e, o=out, i=in_: e.copy(o, i), inc=inc)
    else:
        P.emit(eng, lambda e, o=out, i=in_: e.tensor_copy(o, i), inc=inc)


def t_tensor(P, eng, out, a, b, op, inc=None):
    P.emit(eng, lambda e, o=out, x=a, y=b, p=op: e.tensor_tensor(o, x, y, p), inc=inc)


def t_scalar(P, eng, out, in_, s1, op0, s2=None, op1=None, inc=None):
    if op1 is None:
        P.emit(eng, lambda e, o=out, i=in_, s=s1, p=op0: e.tensor_single_scalar(o, i, s, p), inc=inc)
    else:
        P.emit(eng, lambda e, o=out, i=in_, a=s1, b=s2, p0=op0, p1=op1: e.tensor_scalar(o, i, a, b, p0, p1), inc=inc)


def act(P, out, in_, func, bias=0.0, scale=1.0, accum=None, inc=None):
    P.emit("scalar",
           lambda e, o=out, i=in_, f=func, b=bias, s=scale, a=accum: e.activation(o, i, f, bias=b, scale=s, accum_out=a),
           inc=inc)


def reduce(P, eng, out, in_, op, negate=False, inc=None):
    P.emit(eng, lambda e, o=out, i=in_, p=op, n=negate: e.tensor_reduce(o, i, AX.X, p, negate=n), inc=inc)


def memset(P, eng, ap, v, inc=None):
    P.emit(eng, lambda e, a=ap, x=v: e.memset(a, x), inc=inc)


class CommonCtx:

    def __init__(self, P, mx):
        self.P = P
        self.mx = mx
        self.pav = P.nc.alloc_psum_tensor("cc_pav", [128, 512], F32)
        self.pbc = P.nc.alloc_psum_tensor("cc_pbc", [128, 512], F32)
        self.ptrans = P.nc.alloc_psum_tensor("cc_ptrans", [128, 1024], BF16)
        self.paux = P.nc.alloc_psum_tensor("cc_paux", [128, 512], F32)
        self.ident = P.sbuf("cc_ident", [128, 128], BF16)
        self.identf = P.sbuf("cc_identf", [128, 128], F32)
        self.ones_col = P.sbuf("cc_ones_col", [128, 1], BF16)
        self.ones_row_f = P.sbuf("cc_ones_row_f", [1, 128], F32)
        self.ones_row_b = P.sbuf("cc_ones_row_b", [1, 128], BF16)
        self.s_tr = P.sem("cc_tr")
        self.s_trev = P.sem("cc_trev")
        self.s_bc = P.sem("cc_bc")
        self.s_bcev = P.sem("cc_bcev")
        iot = P.sbuf("cc_iota", [128, 128], I32)
        iotf = P.sbuf("cc_iotf", [128, 128], F32)
        P.emit("gpsimd", lambda e, o=iot[:]: e.iota(o, [[-1, 128]], base=0, channel_multiplier=1,
                                                    allow_small_or_imprecise_dtypes=True))
        memset(P, "vector", self.ones_col[:], 1.0)
        memset(P, "vector", self.ones_row_f[:], 1.0)
        memset(P, "vector", self.ones_row_b[:], 1.0)
        P.barrier()
        t_copy(P, "vector", iotf[:], iot[:])
        P.barrier()
        t_scalar(P, "vector", self.identf[:], iotf[:], 0.0, ALU.is_equal)
        P.barrier()
        t_copy(P, "vector", self.ident[:], self.identf[:])
        P.barrier()

    def transpose(self, out_sbuf_ap, in_sbuf_ap, rows):
        P = self.P
        pt = self.ptrans[0:in_sbuf_ap.shape[1], 0:in_sbuf_ap.shape[0]]
        if self.s_trev.count:
            P.wait("tensor", self.s_trev, self.s_trev.count)
        bp = in_sbuf_ap.base_partition()
        n = in_sbuf_ap.shape[0]
        P.emit("tensor", lambda e, o=pt, i=in_sbuf_ap, d=self.ident[bp:bp + n, bp:bp + n]:
               e.transpose(o, i, d), inc=[(self.s_tr, 1)])
        P.wait("vector", self.s_tr, self.s_tr.count)
        t_copy(P, "vector", out_sbuf_ap, pt, inc=[(self.s_trev, 1)])

    def transpose_f32(self, out_sbuf_ap, in_sbuf_ap, rows):
        P = self.P
        pt = self.paux[0:in_sbuf_ap.shape[1], 0:in_sbuf_ap.shape[0]]
        if self.s_trev.count:
            P.wait("tensor", self.s_trev, self.s_trev.count)
        bp = in_sbuf_ap.base_partition()
        n = in_sbuf_ap.shape[0]
        P.emit("tensor", lambda e, o=pt, i=in_sbuf_ap, d=self.identf[bp:bp + n, bp:bp + n]:
               e.transpose(o, i, d), inc=[(self.s_tr, 1)])
        P.wait("vector", self.s_tr, self.s_tr.count)
        t_copy(P, "vector", out_sbuf_ap, pt, inc=[(self.s_trev, 1)])

    def bcast_rows(self, out_sbuf_ap, row_ap_f32, nrows):
        P = self.P
        F = row_ap_f32.shape[1]
        pb = self.pbc[0:nrows, 0:F]
        if self.s_bcev.count:
            P.wait("tensor", self.s_bcev, self.s_bcev.count)
        P.emit("tensor", lambda e, o=pb, l=self.ones_row_f[0:1, 0:nrows], r=row_ap_f32:
               e.matmul(o, l, r, start=True, stop=True), inc=[(self.s_bc, 1)])
        P.wait("vector", self.s_bc, self.s_bc.count)
        t_copy(P, "vector", out_sbuf_ap, pb, inc=[(self.s_bcev, 1)])


def emit_ln(P, cc, xT, xnT, xn_bf, scr):
    mx = cc.mx
    sqb, sums, sumsq, m2, mean, rstd, meanB, rstdB = (scr[k] for k in
        ("sqb", "sums", "sumsq", "m2", "mean", "rstd", "meanB", "rstdB"))
    xbf = scr.get("xbf", xn_bf)
    for dc in range(DC):
        t_copy(P, "vector" if dc % 2 == 0 else "gpsimd",
               xbf[:, dc * T:(dc + 1) * T], xT[:, dc * T:(dc + 1) * T])
    P.barrier()
    s_sq = P.sem("ln_sq")
    s_sqmm = P.sem("ln_sqmm")
    for tb in range(TB):
        sl = slice(tb * 512, (tb + 1) * 512)
        for dc in range(DC):
            mm(P, cc.paux[0:1, 0:512], cc.ones_col[:],
               xbf[:, dc * T + tb * 512: dc * T + tb * 512 + 512],
               start=(dc == 0), stop=(dc == DC - 1), mmsem=mx.mms if dc == DC - 1 else None)
        P.wait("vector", mx.mms, mx.mms.count)
        t_copy(P, "vector", sums[0:1, sl], cc.paux[0:1, 0:512], inc=[(mx.evs["vector"], 1)])
        mx.wait_evacs("tensor")
        for dc in range(DC):
            half = (dc % 2) * 512
            if dc >= 2:
                P.wait("scalar", s_sqmm, s_sqmm.count - 1)
            act(P, sqb[:, half:half + 512], xbf[:, dc * T + tb * 512: dc * T + tb * 512 + 512],
                AF.Square, inc=[(s_sq, 1)])
            P.wait("tensor", s_sq, s_sq.count)
            mm(P, cc.paux[0:1, 0:512], cc.ones_col[:], sqb[:, half:half + 512],
               start=(dc == 0), stop=(dc == DC - 1), mmsem=s_sqmm)
        P.wait("vector", s_sqmm, s_sqmm.count)
        t_copy(P, "vector", sumsq[0:1, sl], cc.paux[0:1, 0:512], inc=[(mx.evs['vector'], 1)])
        mx.wait_evacs("tensor")
    P.barrier()
    t_scalar(P, "vector", mean[0:1, :], sums[0:1, :], 1.0 / D, ALU.mult)
    P.barrier()
    t_scalar(P, "vector", rstd[0:1, :], sumsq[0:1, :], 1.0 / D, ALU.mult)
    P.barrier()
    t_tensor(P, "vector", m2[0:1, :], mean[0:1, :], mean[0:1, :], ALU.mult)
    P.barrier()
    t_tensor(P, "vector", rstd[0:1, :], rstd[0:1, :], m2[0:1, :], ALU.subtract)
    P.barrier()
    t_scalar(P, "vector", rstd[0:1, :], rstd[0:1, :], 1e-5, ALU.add)
    P.barrier()
    act(P, rstd[0:1, :], rstd[0:1, :], AF.Sqrt)
    P.barrier()
    P.emit("vector", lambda e, o=rstd[0:1, :], i=rstd[0:1, :]: e.reciprocal(o, i))
    P.barrier()
    for tb in range(TB):
        sl = slice(tb * 512, (tb + 1) * 512)
        cc.bcast_rows(meanB[:, sl], mean[0:1, sl], 128)
        cc.bcast_rows(rstdB[:, sl], rstd[0:1, sl], 128)
    P.barrier()
    for dc in range(DC):
        eng = "vector" if dc % 2 == 0 else "gpsimd"
        t_tensor(P, eng, xnT[:, dc * T:(dc + 1) * T], xT[:, dc * T:(dc + 1) * T], meanB[:], ALU.subtract)
    P.barrier()
    for dc in range(DC):
        eng = "vector" if dc % 2 == 0 else "gpsimd"
        t_tensor(P, eng, xnT[:, dc * T:(dc + 1) * T], xnT[:, dc * T:(dc + 1) * T], rstdB[:], ALU.mult)
    P.barrier()
    for dc in range(DC):
        t_copy(P, "scalar" if dc % 2 == 0 else "gpsimd",
               xn_bf[:, dc * T:(dc + 1) * T], xnT[:, dc * T:(dc + 1) * T])
    P.barrier()


def ln_scratch_ar(ar):
    return {
        "sqb": ar.alloc(1024, BF16, "lns_sqb"),
        "sums": ar.alloc(T, F32, "lns_sums"),
        "sumsq": ar.alloc(T, F32, "lns_sumsq"),
        "m2": ar.alloc(T, F32, "lns_m2"),
        "mean": ar.alloc(T, F32, "lns_mean"),
        "rstd": ar.alloc(T, F32, "lns_rstd"),
        "meanB": ar.alloc(T, F32, "lns_meanB"),
        "rstdB": ar.alloc(T, F32, "lns_rstdB"),
    }


def ln_scratch_free(ar, scr):
    for v in scr.values():
        ar.free_(v)



def emit_proj(P, mx, wsl, xn_bf, outT_bf, dout, scale=1.0, eng="scalar"):
    oc = dout // 128
    for o in range(oc):
        for tb in range(TB):
            def ev(e, pap, dst=outT_bf[:, o * T + tb * 512: o * T + tb * 512 + 512], s=scale):
                return e.activation(dst, pap, AF.Copy, bias=0.0, scale=s)
            mx.tile(None, eng, ev,
                    [(wsl(k, o), xn_bf[:, k * T + tb * 512: k * T + tb * 512 + 512]) for k in range(DC)])


def wsl_packed(WT_bf, dout):
    def f(k, o):
        return WT_bf[:, k * dout + o * 128: k * dout + o * 128 + 128]
    return f



def emit_l3_attn(P, cc, ar, xn_bf, sT_out, get_w, mx):
    trimask = ar.alloc(128, F32, "l3trim")
    iot2 = ar.alloc(128, I32, "l3iot")
    dnmt = ar.alloc(512, F32, "l3dnm")
    ones64f = ar.alloc(64, F32, "l3ones")
    obf = ar.alloc(512, BF16, "l3obf")
    Ebuf = ar.alloc(2 * 512, BF16, "l3E")

    P.emit("gpsimd", lambda e, o=iot2[:, 0:128]: e.iota(o, [[-1, 128]], base=0, channel_multiplier=1,
                                                        allow_small_or_imprecise_dtypes=True))
    memset(P, "vector", ones64f[:, :], 1.0)
    P.barrier()
    t_copy(P, "vector", trimask[:, 0:128], iot2[:, 0:128])
    P.barrier()
    t_scalar(P, "vector", trimask[:, 0:128], trimask[:, 0:128], 0.0, ALU.is_gt)
    P.barrier()
    t_scalar(P, "vector", trimask[:, 0:128], trimask[:, 0:128], NEG, ALU.mult)
    P.barrier()

    s_mm = P.sem("l3_mm")
    s_ex = P.sem("l3_ex")
    s_dv = P.sem("l3_dv")
    s_av = P.sem("l3_av")
    s_ms = P.sem("l3_ms")
    sbank_ex = [0, 0]
    eslot_av = [0, 0]

    for half in range(2):
        qT = ar.alloc(4 * T, BF16, "l3qT")
        kT = ar.alloc(4 * T, BF16, "l3kT")
        vtok = ar.alloc(16 * 8 * 65, BF16, "l3vtok")
        vtmp = ar.alloc(512, BF16, "l3vtmp")
        w = get_w("qkv", half)
        emit_proj(P, mx, wsl_packed(w["WqTh"], 512), xn_bf, qT, 512, scale=0.125)
        emit_proj(P, mx, wsl_packed(w["WkTh"], 512), xn_bf, kT, 512)
        P.barrier()
        memset(P, "vector", vtok[:, :], 1.0)
        P.barrier()
        for o in range(4):
            for tb in range(TB):
                if cc.s_tr.count:
                    P.wait("vector", cc.s_tr, cc.s_tr.count)
                def evv(e, pap, dst=vtmp[:, 0:512]):
                    return e.tensor_copy(dst, pap)
                mx.tile(None, "vector", evv,
                        [(w["WvTh"][:, k * 512 + o * 128: k * 512 + o * 128 + 128],
                          xn_bf[:, k * T + tb * 512: k * T + tb * 512 + 512]) for k in range(DC)])
                mx.wait_evacs("tensor")
                for hh2 in range(2):
                    hh = o * 2 + hh2
                    for tt in range(4):
                        tile = tb * 4 + tt
                        cc.transpose(vtok[:, (tile * 8 + hh) * 65: (tile * 8 + hh) * 65 + 64],
                                     vtmp[hh2 * 64:hh2 * 64 + 64, tt * 128:tt * 128 + 128], 64)
        P.barrier()
        get_w("free_qkv", half)
        Obuf = ar.alloc(8 * T, BF16, "l3O")
        for hh in range(8):
            ch, hb = hh // 2, (hh % 2) * 64
            qsl = lambda c0, c1: qT[hb:hb + 64, ch * T + c0: ch * T + c1]
            ksl = lambda c0, c1: kT[hb:hb + 64, ch * T + c0: ch * T + c1]
            for qb in range(TB):
                nst = 4 * qb + 4
                for st in range(nst):
                    sb = st % 2
                    spb = mx.banks[sb]
                    if sbank_ex[sb]:
                        P.wait("tensor", s_ex, sbank_ex[sb])
                    mm(P, spb[:, 0:512], ksl(st * 128, st * 128 + 128), qsl(qb * 512, qb * 512 + 512),
                       True, True, mmsem=s_mm)
                    if 4 * qb <= st:
                        c0 = st * 128 - qb * 512
                        P.wait("vector", s_mm, s_mm.count)
                        if c0 > 0:
                            memset(P, "vector", spb[:, 0:c0], NEG)
                        t_tensor(P, "vector", spb[:, c0:c0 + 128], spb[:, c0:c0 + 128],
                                 trimask[:, 0:128], ALU.add, inc=[(s_dv, 1)])
                        P.wait("scalar", s_dv, s_dv.count)
                    else:
                        P.wait("scalar", s_mm, s_mm.count)
                    if eslot_av[sb]:
                        P.wait("scalar", s_av, eslot_av[sb])
                    act(P, Ebuf[:, sb * 512:(sb + 1) * 512], spb[:, 0:512], AF.Exp, inc=[(s_ex, 1)])
                    sbank_ex[sb] = s_ex.count
                    P.wait("tensor", s_ex, s_ex.count)
                    mm(P, cc.pav[0:65, 0:512], vtok[:, (st * 8 + hh) * 65: (st * 8 + hh) * 65 + 65],
                       Ebuf[:, sb * 512:(sb + 1) * 512], start=(st == 0), stop=(st == nst - 1),
                       mmsem=s_av)
                    eslot_av[sb] = s_av.count
                P.wait("vector", s_av, s_av.count)
                t_scalar(P, "vector", dnmt[64:65, :], cc.pav[64:65, 0:512], 1e-8, ALU.add)
                P.barrier()
                P.emit("vector", lambda e, o=dnmt[64:65, :], i=dnmt[64:65, :]: e.reciprocal(o, i))
                act(P, obf[0:64, :], cc.pav[0:64, 0:512], AF.Copy)
                P.barrier()
                P.emit("tensor", lambda e, o=cc.pbc[0:64, 0:512], l=ones64f[64:65, 0:64], r=dnmt[64:65, :]:
                       e.matmul(o, l, r, start=True, stop=True), inc=[(s_ms, 1)])
                P.wait("vector", s_ms, s_ms.count)
                t_tensor(P, "vector", Obuf[0:64, hh * T + qb * 512: hh * T + qb * 512 + 512],
                         obf[0:64, :], cc.pbc[0:64, 0:512], ALU.mult)
                P.barrier()
        if half == 0 and hasattr(P, "_dbg_q"):
            P.dma("sync", P._dbg_q.rearrange("(c p) t -> p c t", p=128)[:, 0:4, :],
                  qT[:].rearrange("p (c t) -> p c t", c=4))
            P.dma("sync", P._dbg_q.rearrange("(c p) t -> p c t", p=128)[:, 4:8, :],
                  kT[:].rearrange("p (c t) -> p c t", c=4))
            P.dma("sync", P._dbg_o.rearrange("(c p) t -> p c t", p=128)[0:64, 0:8, :],
                  Obuf[0:64, :].rearrange("p (c t) -> p c t", c=8))
            P.barrier()
        ar.free_(qT); ar.free_(kT); ar.free_(vtok); ar.free_(vtmp)
        WoTh = get_w("wo", half)
        for o in range(DC):
            for tb in range(TB):
                dst = sT_out[:, o * T + tb * 512: o * T + tb * 512 + 512]
                if half == 0:
                    def evo(e, pap, d=dst):
                        return e.tensor_copy(d, pap)
                else:
                    def evo(e, pap, d=dst):
                        return e.tensor_tensor(d, d, pap, ALU.add)
                mx.tile(None, "vector", evo,
                        [(WoTh[0:64, hh2 * 1024 + o * 128: hh2 * 1024 + o * 128 + 128],
                          Obuf[0:64, hh2 * T + tb * 512: hh2 * T + tb * 512 + 512]) for hh2 in range(8)])
        P.barrier()
        ar.free_(Obuf)
        get_w("free_wo", half)
    for buf in (trimask, iot2, dnmt, ones64f, obf, Ebuf):
        ar.free_(buf)


def build_l3_attn_test():
    P = Prog("l3_test")
    nc = P.nc
    XN = nc.dram_tensor("XN", [D, T], F32, kind="ExternalInput").ap()
    WQT = nc.dram_tensor("WQT", [D, D], F32, kind="ExternalInput").ap()
    WKVT = nc.dram_tensor("WKVT", [D, 2 * D], F32, kind="ExternalInput").ap()
    WOTH = nc.dram_tensor("WOTH", [64, 16 * 1024], F32, kind="ExternalInput").ap()
    OUT = nc.dram_tensor("OUT", [D, T], BF16, kind="ExternalOutput").ap()
    OUTQ = nc.dram_tensor("OUTQ", [1024, T], BF16, kind="ExternalOutput").ap()
    OUTO = nc.dram_tensor("OUTO", [1024, T], BF16, kind="ExternalOutput").ap()
    P._dbg_q = OUTQ
    P._dbg_o = OUTO

    ar = Arena(P, 200 * 1024)
    xnb = ar.alloc(DC * T, BF16, "xnb")
    sT = ar.alloc(DC * T, BF16, "sT")
    P.dma("gpsimd", xnb[:].rearrange("p (c t) -> p c t", c=DC), XN.rearrange("(c p) t -> p c t", p=128))
    P.barrier()
    mx = MatmulCtx(P, 4)
    cc = CommonCtx(P, mx)
    state = {}

    def get_w(phase, half):
        if phase == "qkv":
            state["WqTh"] = ar.alloc(DC * 512, BF16, "wqth")
            state["WkTh"] = ar.alloc(DC * 512, BF16, "wkth")
            state["WvTh"] = ar.alloc(DC * 512, BF16, "wvth")
            P.dma("gpsimd", state["WqTh"][:].rearrange("p (c n) -> p c n", c=DC),
                  WQT.rearrange("(c p) n -> p c n", p=128)[:, :, half * 512:half * 512 + 512])
            P.dma("gpsimd", state["WkTh"][:].rearrange("p (c n) -> p c n", c=DC),
                  WKVT.rearrange("(c p) n -> p c n", p=128)[:, :, half * 512:half * 512 + 512])
            P.dma("gpsimd", state["WvTh"][:].rearrange("p (c n) -> p c n", c=DC),
                  WKVT.rearrange("(c p) n -> p c n", p=128)[:, :, 1024 + half * 512:1024 + half * 512 + 512])
            P.barrier()
            return state
        if phase == "free_qkv":
            ar.free_(state["WqTh"]); ar.free_(state["WkTh"]); ar.free_(state["WvTh"])
            return None
        if phase == "wo":
            woth = ar.alloc(8 * 1024, BF16, "woth")
            state["woth"] = woth
            P.dma("gpsimd", woth[0:64, :], WOTH[:, half * 8192:half * 8192 + 8192])
            P.barrier()
            return woth
        if phase == "free_wo":
            ar.free_(state["woth"])
            return None

    emit_l3_attn(P, cc, ar, xnb, sT, get_w, mx)
    P.dma("sync", OUT.rearrange("(c p) t -> p c t", p=128), sT[:].rearrange("p (c t) -> p c t", c=DC))
    P.barrier()
    return P.finalize()



def emit_l0_attn(P, cc, ar, xn_bf, sT_out, get_w, mx):
    C = get_w("consts")
    trimask = ar.alloc(128, F32, "l0trim")
    iot = ar.alloc(128, I32, "l0iot")
    dnmt = ar.alloc(2048, F32, "l0dnm")
    ones64f = ar.alloc(64, F32, "l0ones")
    decayexp = ar.alloc(16, F32, "l0dexp")
    lmcol = ar.alloc(16, F32, "l0lm")
    MT = ar.alloc(32, BF16, "l0MT")
    MTf = ar.alloc(32, F32, "l0MTf")
    twB = ar.alloc(T, F32, "l0twB")
    twrow = ar.alloc(T, F32, "l0twrow")
    Ebuf = ar.alloc(T, BF16, "l0E")
    pcsb = ar.alloc(T, F32, "l0pcs")
    pcsbf = ar.alloc(T, BF16, "l0pcsbf")
    wbufw = ar.alloc(T, F32, "l0w")
    wbf = ar.alloc(T, BF16, "l0wbf")
    wT = ar.alloc(16 * 32, F32, "l0wT")
    ktok = ar.alloc(16 * 64, BF16, "l0ktok")
    vtok = ar.alloc(16 * 65, BF16, "l0vtok")
    kvb = ar.alloc(32 * 64, BF16, "l0kvb")
    sker = ar.alloc(64, BF16, "l0sker")
    outt = ar.alloc(T, BF16, "l0outt")
    outl = ar.alloc(T, BF16, "l0outl")
    cbuf = ar.alloc(1, F32, "l0c")
    cblk = ar.alloc(32, BF16, "l0cblk")
    rden = ar.alloc(1, F32, "l0rden")
    obf2 = ar.alloc(128, BF16, "l0obf")

    s_mm = P.sem("l0_mm")
    s_ex = P.sem("l0_ex")
    s_dv = P.sem("l0_dv")
    s_pc = P.sem("l0_pc")
    s_pcev = P.sem("l0_pcev")
    s_kv = P.sem("l0_kv")
    s_kvm = P.sem("l0_kvm")
    s_ms = P.sem("l0_ms")
    triu = ar.alloc(128, F32, "l0triu")
    ar_cblk_bf = ar.alloc(32, BF16, "l0cblkbf")
    masknegT = ar.alloc(32, F32, "l0maskneg")

    P.emit("gpsimd", lambda e, o=iot[:, 0:128]: e.iota(o, [[-1, 128]], base=0, channel_multiplier=1,
                                                       allow_small_or_imprecise_dtypes=True))
    memset(P, "vector", ones64f[:, :], 1.0)
    P.barrier()
    t_copy(P, "vector", trimask[:, 0:128], iot[:, 0:128])
    P.barrier()
    t_scalar(P, "vector", trimask[:, 0:128], trimask[:, 0:128], 0.0, ALU.is_gt)
    P.barrier()
    t_scalar(P, "vector", trimask[:, 0:128], trimask[:, 0:128], NEG, ALU.mult)
    t_copy(P, "gpsimd", triu[:, 0:128], iot[:, 0:128])
    P.barrier()
    t_scalar(P, "vector", triu[:, 0:128], triu[:, 0:128], 0.0, ALU.is_lt)
    P.barrier()
    t_scalar(P, "vector", triu[:, 0:128], triu[:, 0:128], NEG, ALU.mult)
    act(P, decayexp[:, :], C["decayexp_src"][:, :], AF.Exp)
    t_scalar(P, "vector", masknegT[0:32, 0:32], C["maskT"][0:32, 0:32], -1.0, ALU.add)
    P.barrier()
    t_scalar(P, "vector", masknegT[0:32, 0:32], masknegT[0:32, 0:32], 1.0e30, ALU.mult)
    P.barrier()
    t_scalar(P, "vector", lmcol[0:32, :], C["mdrep"][0:32, :], 64.0, ALU.mult)
    P.barrier()
    t_scalar(P, "vector", lmcol[0:32, :], lmcol[0:32, :], 0.6931471805599453, ALU.min)
    act(P, twrow[0:1, 0:64], C["tmrow"][0:1, 0:64], AF.Sigmoid)
    P.barrier()
    P.emit("vector", lambda e, o=twrow[0:1, :].rearrange("p (a b) -> p a b", a=32),
           i=twrow[0:1, 0:64].rearrange("p (a b) -> p a b", a=1).broadcast_to([1, 32, 64]):
           e.tensor_copy(o, i))
    P.barrier()
    cc.bcast_rows(twB[:, 0:512], twrow[0:1, 0:512], 128)
    cc.bcast_rows(twB[:, 512:1024], twrow[0:1, 512:1024], 128)
    cc.bcast_rows(twB[:, 1024:1536], twrow[0:1, 1024:1536], 128)
    cc.bcast_rows(twB[:, 1536:2048], twrow[0:1, 1536:2048], 128)
    P.barrier()

    for half in range(4):
        qT = ar.alloc(2 * T, BF16, "l0qT")
        kT = ar.alloc(2 * T, BF16, "l0kT")
        vT = ar.alloc(2 * T, BF16, "l0vT")
        w = get_w("qkv", half)
        emit_proj(P, mx, wsl_packed(w["WqTh"], 256), xn_bf, qT, 256, scale=0.125)
        emit_proj(P, mx, wsl_packed(w["WkTh"], 256), xn_bf, kT, 256)
        emit_proj(P, mx, wsl_packed(w["WvTh"], 256), xn_bf, vT, 256)
        P.barrier()
        get_w("free_qkv", half)
        Obuf = ar.alloc(4 * T, BF16, "l0O")
        for hh in range(4):
            h = half * 4 + hh
            ch, bh = hh // 2, (hh % 2) * 64
            qsl = lambda c0, c1: qT[bh:bh + 64, ch * T + c0: ch * T + c1]
            ksl = lambda c0, c1: kT[bh:bh + 64, ch * T + c0: ch * T + c1]
            vsl = lambda c0, c1: vT[bh:bh + 64, ch * T + c0: ch * T + c1]
            memset(P, "vector", vtok[:, :], 1.0)
            P.barrier()
            for st in range(16):
                cc.transpose(vtok[:, st * 65: st * 65 + 64], vsl(st * 128, st * 128 + 128), 64)
                cc.transpose(ktok[:, st * 64: st * 64 + 64], ksl(st * 128, st * 128 + 128), 64)
            P.barrier()
            t_scalar(P, "vector", MTf[0:32, 0:32], C["ematT"][0:32, 0:32], lmcol[0:32, h:h + 1], ALU.mult)
            P.barrier()
            t_tensor(P, "vector", MTf[0:32, 0:32], MTf[0:32, 0:32], C["maskT"][0:32, 0:32], ALU.mult)
            P.barrier()
            t_tensor(P, "vector", MTf[0:32, 0:32], MTf[0:32, 0:32], masknegT[0:32, 0:32], ALU.add)
            P.barrier()
            act(P, MTf[0:32, 0:32], MTf[0:32, 0:32], AF.Exp)
            P.barrier()
            t_copy(P, "vector", MT[0:32, 0:32], MTf[0:32, 0:32])
            memset(P, "vector", pcsb[0:32, :], 0.0)
            memset(P, "vector", cblk[:, 0:32], 0.0)
            P.barrier()
            sbank_ex = [0, 0]
            pcsbank_ev = [0, 0]
            for qt in range(16):
                q0 = qt * 128
                nsb = qt // 4 + 1
                for sb in range(nsb):
                    spb = mx.banks[sb % 2]
                    if sbank_ex[sb % 2]:
                        P.wait("tensor", s_ex, sbank_ex[sb % 2])
                    mm(P, spb[:, 0:512], qsl(q0, q0 + 128), ksl(sb * 512, sb * 512 + 512),
                       True, True, mmsem=s_mm)
                    c0 = q0 - sb * 512
                    P.wait("vector", s_mm, s_mm.count)
                    if 0 <= c0 < 512:
                        if c0 + 128 < 512:
                            memset(P, "vector", spb[:, c0 + 128:512], NEG)
                        t_tensor(P, "vector", spb[:, c0:c0 + 128], spb[:, c0:c0 + 128],
                                 triu[:, 0:128], ALU.add, inc=[(s_dv, 1)])
                        P.wait("scalar", s_dv, s_dv.count)
                    else:
                        P.wait("scalar", s_mm, s_mm.count)
                    act(P, Ebuf[:, sb * 512:(sb + 1) * 512], spb[:, 0:512], AF.Exp,
                        accum=dnmt[:, 4 * (qt % 2) + sb: 4 * (qt % 2) + sb + 1],
                        inc=[(s_ex, 1)])
                    sbank_ex[sb % 2] = s_ex.count
                P.wait("vector", s_ex, s_ex.count)
                reduce(P, "vector", rden[:, 0:1], dnmt[:, 4 * (qt % 2): 4 * (qt % 2) + nsb], ALU.add)
                P.barrier()
                P.emit("vector", lambda e, o=rden[:, 0:1], i=rden[:, 0:1]: e.reciprocal(o, i))
                P.barrier()
                t_tensor(P, "vector", cbuf[:, 0:1], decayexp[:, h:h + 1], rden[:, 0:1], ALU.mult)
                P.barrier()
                if qt > 0:
                    memset(P, "vector", cblk[:, 2 * (qt - 1):2 * qt], 0.0)
                    P.barrier()
                t_copy(P, "vector", cblk[0:64, 2 * qt:2 * qt + 1], cbuf[0:64, 0:1])
                t_copy(P, "vector", cblk[64:128, 2 * qt + 1:2 * qt + 2], cbuf[64:128, 0:1])
                P.barrier()
                cbb = ar_cblk_bf
                t_copy(P, "vector", cbb[:, 0:32], cblk[:, 0:32])
                P.barrier()
                for sb in range(nsb):
                    pcb = mx.banks[2 + sb % 2]
                    if pcsbank_ev[sb % 2]:
                        P.wait("tensor", s_pcev, pcsbank_ev[sb % 2])
                    mm(P, pcb[0:32, 0:512], cbb[:, 0:32], Ebuf[:, sb * 512:(sb + 1) * 512],
                       True, True, mmsem=s_pc)
                    P.wait("vector", s_pc, s_pc.count)
                    t_tensor(P, "vector", pcsb[0:32, sb * 512:(sb + 1) * 512],
                             pcsb[0:32, sb * 512:(sb + 1) * 512], pcb[0:32, 0:512], ALU.add,
                             inc=[(s_pcev, 1)])
                    pcsbank_ev[sb % 2] = s_pcev.count
                entries = [c for c in (2 * qt, 2 * qt - 1, 2 * qt + 1) if c >= 0]
                for ci, c64 in enumerate(entries):
                    eb = (c64 % 2) * 64
                    qrows = 64 if c64 == 2 * qt - 1 else 128
                    pt = cc.ptrans[eb:eb + 64, 0:qrows]
                    if cc.s_trev.count:
                        P.wait("tensor", cc.s_trev, cc.s_trev.count)
                    P.emit("tensor", lambda e, o=pt, i=Ebuf[0:qrows, c64 * 64:(c64 + 1) * 64],
                           d=cc.ident[0:qrows, 0:qrows]: e.transpose(o, i, d), inc=[(cc.s_tr, 1)])
                    P.wait("vector", cc.s_tr, cc.s_tr.count)
                    t_copy(P, "vector", obf2[eb:eb + 64, 0:qrows], pt, inc=[(cc.s_trev, 1)])
                    P.wait("tensor", cc.s_trev, cc.s_trev.count)
                    tile, half64 = c64 // 2, (c64 % 2) * 64
                    mm(P, cc.pbc[0:65, 0:qrows], vtok[half64:half64 + 64, tile * 65: tile * 65 + 65],
                       obf2[eb:eb + 64, 0:qrows], start=(ci == 0), stop=(ci == len(entries) - 1),
                       mmsem=s_ms if ci == len(entries) - 1 else None)
                P.wait("vector", s_ms, s_ms.count)
                t_scalar(P, "vector", dnmt[64:65, 1024:1152], cc.pbc[64:65, 0:128], 1e-30, ALU.add)
                P.barrier()
                P.emit("vector", lambda e, o=dnmt[64:65, 1024:1152], i=dnmt[64:65, 1024:1152]: e.reciprocal(o, i))
                act(P, obf2[0:64, 0:128], cc.pbc[0:64, 0:128], AF.Copy)
                P.barrier()
                P.emit("tensor", lambda e, o=cc.pbc[0:64, 0:128], l=ones64f[64:65, 0:64],
                       r=dnmt[64:65, 1024:1152]: e.matmul(o, l, r, start=True, stop=True), inc=[(s_ms, 1)])
                P.wait("vector", s_ms, s_ms.count)
                t_tensor(P, "vector", outt[0:64, q0:q0 + 128], obf2[0:64, 0:128], cc.pbc[0:64, 0:128],
                         ALU.mult)
                P.barrier()
            for dc4 in range(4):
                t_copy(P, "vector", pcsbf[0:32, dc4 * 512:(dc4 + 1) * 512], pcsb[0:32, dc4 * 512:(dc4 + 1) * 512])
            P.barrier()
            for dc4 in range(4):
                pcb = mx.banks[dc4 % 2]
                mm(P, pcb[0:32, 0:512], MT[0:32, 0:32], pcsbf[0:32, dc4 * 512:(dc4 + 1) * 512],
                   True, True, mmsem=s_pc)
                P.wait("vector", s_pc, s_pc.count)
                t_copy(P, "vector", wbufw[0:32, dc4 * 512:(dc4 + 1) * 512], pcb[0:32, 0:512],
                       inc=[(s_pcev, 1)])
                P.wait("tensor", s_pcev, s_pcev.count)
            P.barrier()
            reduce(P, "vector", rden[0:32, 0:1], wbufw[0:32, :], ALU.add)
            P.barrier()
            t_scalar(P, "vector", rden[0:32, 0:1], rden[0:32, 0:1], 1e-5, ALU.add)
            P.barrier()
            P.emit("vector", lambda e, o=rden[0:32, 0:1], i=rden[0:32, 0:1]: e.reciprocal(o, i))
            P.barrier()
            t_scalar(P, "vector", wbufw[0:32, :], wbufw[0:32, :], rden[0:32, 0:1], ALU.mult)
            P.barrier()
            t_copy(P, "vector", wbf[0:32, :], wbufw[0:32, :])
            P.barrier()
            for st in range(16):
                cc.transpose(wT[:, st * 32:(st + 1) * 32], wbf[0:32, st * 128:(st + 1) * 128], 32)
            P.barrier()
            for r in range(32):
                nkeys = 64 * r
                if nkeys == 0:
                    memset(P, "vector", kvb[bh:bh + 64, 0:64], 0.0)
                    continue
                nch = (nkeys + 127) // 128
                for kc in range(nch):
                    rows = min(128, nkeys - kc * 128)
                    if s_kvm.count:
                        P.wait("vector", s_kvm, s_kvm.count)
                    t_scalar(P, "vector", sker[0:rows, 0:64], ktok[0:rows, kc * 64: kc * 64 + 64],
                             wT[0:rows, kc * 32 + r: kc * 32 + r + 1],
                             ALU.mult, inc=[(s_dv, 1)])
                    P.wait("tensor", s_dv, s_dv.count)
                    mm(P, mx.banks[3][bh:bh + 64, 0:64], sker[0:rows, 0:64],
                       vtok[0:rows, kc * 65:kc * 65 + 64],
                       start=(kc == 0), stop=(kc == nch - 1), mmsem=s_kvm)
                P.wait("vector", s_kvm, s_kvm.count)
                t_scalar(P, "vector", kvb[bh:bh + 64, r * 64:(r + 1) * 64], mx.banks[3][bh:bh + 64, 0:64],
                         8.0, ALU.mult, inc=[(s_kv, 1)])
                P.wait("tensor", s_kv, s_kv.count)
            P.barrier()
            for qt in range(16):
                for rh in range(2):
                    r = 2 * qt + rh
                    mm(P, cc.pbc[0:64, rh * 64:rh * 64 + 64], kvb[bh:bh + 64, r * 64:(r + 1) * 64],
                       qsl(qt * 128 + rh * 64, qt * 128 + rh * 64 + 64),
                       True, True, mmsem=s_ms if rh == 1 else None)
                P.wait("vector", s_ms, s_ms.count)
                t_copy(P, "vector", outl[0:64, qt * 128:(qt + 1) * 128], cc.pbc[0:64, 0:128],
                       inc=[(s_dv, 1)])
                P.wait("tensor", s_dv, s_dv.count)
            P.barrier()
            if h == 0 and hasattr(P, "_l0dbg"):
                P.dma("sync", P._l0dbg["pcs"], pcsb[0:32, :])
                P.dma("sync", P._l0dbg["w"], wbufw[0:32, :])
                P.dma("sync", P._l0dbg["outt"], outt[0:64, :])
                P.dma("sync", P._l0dbg["outl"], outl[0:64, :])
                P.barrier()
            for tb2 in range(2):
                sl2 = slice(tb2 * 1024, (tb2 + 1) * 1024)
                t_tensor(P, "vector", outt[0:64, sl2], outt[0:64, sl2], outl[0:64, sl2], ALU.subtract)
            P.barrier()
            for tb2 in range(2):
                sl2 = slice(tb2 * 1024, (tb2 + 1) * 1024)
                t_tensor(P, "vector", outt[0:64, sl2], outt[0:64, sl2], twB[0:64, sl2], ALU.mult)
            P.barrier()
            for tb2 in range(2):
                sl2 = slice(tb2 * 1024, (tb2 + 1) * 1024)
                t_tensor(P, "vector", Obuf[0:64, hh * T + tb2 * 1024: hh * T + (tb2) * 1024 + 1024],
                         outl[0:64, sl2], outt[0:64, sl2], ALU.add)
            P.barrier()
        ar.free_(qT); ar.free_(kT); ar.free_(vT)
        WcTh = get_w("wc", half)
        for o in range(DC):
            for tb in range(TB):
                dst = sT_out[:, o * T + tb * 512: o * T + tb * 512 + 512]
                if half == 0:
                    def evo(e, pap, d=dst):
                        return e.tensor_copy(d, pap)
                else:
                    def evo(e, pap, d=dst):
                        return e.tensor_tensor(d, d, pap, ALU.add)
                mx.tile(None, "vector", evo,
                        [(WcTh[0:64, hh2 * 1024 + o * 128: hh2 * 1024 + o * 128 + 128],
                          Obuf[0:64, hh2 * T + tb * 512: hh2 * T + tb * 512 + 512]) for hh2 in range(4)])
        P.barrier()
        ar.free_(Obuf)
        get_w("free_wc", half)
    for b in (trimask, iot, dnmt, ones64f, decayexp, lmcol, MT, MTf, twB, twrow, Ebuf,
              pcsb, pcsbf, wbufw, wbf, wT, ktok, vtok, kvb, sker, outt, outl, cbuf, cblk,
              rden, obf2, triu, ar_cblk_bf, masknegT):
        ar.free_(b)


def build_l0_attn_test():
    P = Prog("l0_test")
    nc = P.nc
    XN = nc.dram_tensor("XN", [D, T], F32, kind="ExternalInput").ap()
    WQT = nc.dram_tensor("WQT", [D, D], F32, kind="ExternalInput").ap()
    WKT = nc.dram_tensor("WKT", [D, D], F32, kind="ExternalInput").ap()
    WVT = nc.dram_tensor("WVT", [D, D], F32, kind="ExternalInput").ap()
    WCTH = nc.dram_tensor("WCTH", [64, 16 * 1024], F32, kind="ExternalInput").ap()
    DEXP = nc.dram_tensor("DEXP", [128, 16], F32, kind="ExternalInput").ap()
    MDREP = nc.dram_tensor("MDREP", [32, 16], F32, kind="ExternalInput").ap()
    EMATT = nc.dram_tensor("EMATT", [32, 32], F32, kind="ExternalInput").ap()
    MASKT = nc.dram_tensor("MASKT", [32, 32], F32, kind="ExternalInput").ap()
    TMROW = nc.dram_tensor("TMROW", [1, 64], F32, kind="ExternalInput").ap()
    OUT = nc.dram_tensor("OUT", [D, T], BF16, kind="ExternalOutput").ap()
    P._l0dbg = {"pcs": nc.dram_tensor("DBGPCS", [32, T], F32, kind="ExternalOutput").ap(),
                "w": nc.dram_tensor("DBGW", [32, T], F32, kind="ExternalOutput").ap(),
                "outt": nc.dram_tensor("DBGOT", [64, T], BF16, kind="ExternalOutput").ap(),
                "outl": nc.dram_tensor("DBGOL", [64, T], BF16, kind="ExternalOutput").ap()}

    ar = Arena(P, 200 * 1024)
    xnb = ar.alloc(DC * T, BF16, "xnb")
    sT = ar.alloc(DC * T, BF16, "sT")
    cst = {"decayexp_src": ar.alloc(16, F32, "c_dexp"), "mdrep": ar.alloc(16, F32, "c_md"),
           "ematT": ar.alloc(32, F32, "c_em"), "maskT": ar.alloc(32, F32, "c_mk"),
           "tmrow": ar.alloc(64, F32, "c_tm")}
    P.dma("gpsimd", xnb[:].rearrange("p (c t) -> p c t", c=DC), XN.rearrange("(c p) t -> p c t", p=128))
    P.dma("sync", cst["decayexp_src"][:, :], DEXP)
    P.dma("sync", cst["mdrep"][0:32, :], MDREP)
    P.dma("sync", cst["ematT"][0:32, :], EMATT)
    P.dma("sync", cst["maskT"][0:32, :], MASKT)
    P.dma("sync", cst["tmrow"][0:1, 0:64], TMROW)
    P.barrier()
    mx = MatmulCtx(P, 4)
    cc = CommonCtx(P, mx)
    state = {}

    def get_w(phase, half=None):
        if phase == "consts":
            return cst
        if phase == "qkv":
            state["WqTh"] = ar.alloc(DC * 256, BF16, "wqth")
            state["WkTh"] = ar.alloc(DC * 256, BF16, "wkth")
            state["WvTh"] = ar.alloc(DC * 256, BF16, "wvth")
            for nm, dr in (("WqTh", WQT), ("WkTh", WKT), ("WvTh", WVT)):
                P.dma("gpsimd", state[nm][:].rearrange("p (c n) -> p c n", c=DC),
                      dr.rearrange("(c p) n -> p c n", p=128)[:, :, half * 256:half * 256 + 256])
            P.barrier()
            return state
        if phase == "free_qkv":
            ar.free_(state["WqTh"]); ar.free_(state["WkTh"]); ar.free_(state["WvTh"])
            return None
        if phase == "wc":
            wcth = ar.alloc(4 * 1024, BF16, "wcth")
            state["wcth"] = wcth
            P.dma("gpsimd", wcth[0:64, :], WCTH[:, half * 4096:half * 4096 + 4096])
            P.barrier()
            return wcth
        if phase == "free_wc":
            ar.free_(state["wcth"])
            return None

    emit_l0_attn(P, cc, ar, xnb, sT, get_w, mx)
    P.dma("sync", OUT.rearrange("(c p) t -> p c t", p=128), sT[:].rearrange("p (c t) -> p c t", c=DC))
    P.barrier()
    return P.finalize()



L2D = 6
H1B = 2730
L2FC = [(i * 128, min(128, H1B - i * 128)) for i in range((H1B + 127) // 128)]


def emit_l2_ffn(P, cc, ar, xnT_bf, finish_cb, get_w, mx, free_xn=False):
    QKS = 8.0
    x2 = ar.alloc(DC * T, BF16, "l2f_x2")
    xs = ar.alloc(DC * T, BF16, "l2f_xs")
    Ebuf = ar.alloc(512, BF16, "l2f_E")
    mq16 = ar.alloc(16, F32, "l2f_mq16")
    onesrow_b = ar.alloc(128, BF16, "l2f_onesrow")
    dnm2 = ar.alloc(256, F32, "l2f_dnm")
    ones64b = ar.alloc(64, F32, "l2f_ones")
    obf = ar.alloc(256, BF16, "l2f_obf")
    rotc = get_w("rotc")
    prot = get_w("prot")
    h2col = get_w("h2col")
    h2row = get_w("h2row")
    amask = get_w("amask")

    s_mm = P.sem("l2f_mm")
    s_ex = P.sem("l2f_ex")
    s_dv = P.sem("l2f_dv")
    s_av = P.sem("l2f_av")
    s_ms = P.sem("l2f_ms")
    sbank_ex2 = [0, 0]
    eslot_av2 = [0]
    mqdram = get_w("mqdram")
    memset(P, "vector", ones64b[:, :], 1.0)
    memset(P, "vector", onesrow_b[0:1, :], 1.0)
    for dc in range(DC):
        t_copy(P, "vector" if dc % 2 == 0 else "gpsimd", x2[:, dc * T:(dc + 1) * T],
               xnT_bf[:, dc * T:(dc + 1) * T])
    P.barrier()
    if free_xn:
        ar.free_(xnT_bf)

    def shift_into(dst, srcbuf):
        for dc in range(DC):
            eng = "vector" if dc % 2 == 0 else "gpsimd"
            if dc < 4:
                t_copy(P, eng, dst[:, dc * T:(dc + 1) * T], srcbuf[:, dc * T:(dc + 1) * T])
            else:
                memset(P, eng, dst[:, dc * T:dc * T + 1], 0.0)
                t_copy(P, eng, dst[:, dc * T + 1:(dc + 1) * T], srcbuf[:, dc * T:(dc + 1) * T - 1])
        P.barrier()

    for li in range(L2D):
        q2 = ar.alloc(T, BF16, "l2f_q2")
        k2 = ar.alloc(T, BF16, "l2f_k2")
        v2 = ar.alloc(T, BF16, "l2f_v2")
        pq = ar.alloc(T, BF16, "l2f_pq")
        t1 = ar.alloc(T, BF16, "l2f_t1")
        vtok = ar.alloc(16 * 65, BF16, "l2f_vtok")
        Obuf = ar.alloc(4 * T, BF16, "l2f_O")
        rnB = ar.alloc(T, F32, "l2f_rnB")
        nrm = ar.alloc(T, F32, "l2f_nrm")
        mqrow = ar.alloc(T, F32, "l2f_mqrow")
        mqrowb = ar.alloc(T, BF16, "l2f_mqrowb")
        csq = ar.alloc(T, BF16, "l2f_csq")
        ssq = ar.alloc(T, BF16, "l2f_ssq")
        csk = ar.alloc(T, BF16, "l2f_csk")
        ssk = ar.alloc(T, BF16, "l2f_ssk")
        shift_into(xs, x2)
        qsB, ksB = get_w("scales", li)
        for tb in range(TB):
            sl = slice(tb * 512, (tb + 1) * 512)
            t_scalar(P, "vector", csq[:, sl], rotc["csq"][:, sl], qsB, ALU.mult)
            t_scalar(P, "gpsimd", ssq[:, sl], rotc["ssq"][:, sl], qsB, ALU.mult)
            t_scalar(P, "vector", csk[:, sl], rotc["csk"][:, sl], ksB, ALU.mult)
            t_scalar(P, "gpsimd", ssk[:, sl], rotc["ssk"][:, sl], ksB, ALU.mult)
        P.barrier()
        for grp in range(4):
            for pp in range(2):
                pair = grp * 2 + pp
                wq, wk, wv = get_w("qkv", li, pair)
                emit_proj(P, mx, wsl_packed(wq, 128), xs, q2, 128)
                emit_proj(P, mx, wsl_packed(wk, 128), xs, k2, 128)
                emit_proj(P, mx, wsl_packed(wv, 128), xs, v2, 128)
                P.barrier()
                for which, buf in (("q", q2), ("k", k2)):
                    act(P, t1[:, :], buf[:, :], AF.Square)
                    P.barrier()
                    for tb in range(TB):
                        mm(P, cc.paux[0:2, 0:512],
                           h2col[0:128, 0:2], t1[:, tb * 512:(tb + 1) * 512], True, True, mmsem=mx.mms)
                        P.wait("vector", mx.mms, mx.mms.count)
                        t_copy(P, "vector", nrm[0:2, tb * 512:(tb + 1) * 512], cc.paux[0:2, 0:512],
                               inc=[(mx.evs["vector"], 1)])
                        mx.wait_evacs("tensor")
                    P.barrier()
                    act(P, nrm[0:2, :], nrm[0:2, :], AF.Sqrt)
                    P.barrier()
                    t_scalar(P, "vector", nrm[0:2, :], nrm[0:2, :], 1e-12, ALU.max)
                    P.barrier()
                    P.emit("vector", lambda e, o=nrm[0:2, :], i=nrm[0:2, :]: e.reciprocal(o, i))
                    P.barrier()
                    if which == "q":
                        t_scalar(P, "vector", nrm[0:2, :], nrm[0:2, :], QKS, ALU.mult)
                        P.barrier()
                    for tb in range(TB):
                        if cc.s_bcev.count:
                            P.wait("tensor", cc.s_bcev, cc.s_bcev.count)
                        P.emit("tensor", lambda e, o=cc.pbc[:, 0:512], l=h2row[0:2, 0:128],
                               r=nrm[0:2, tb * 512:(tb + 1) * 512]: e.matmul(o, l, r, start=True, stop=True),
                               inc=[(cc.s_bc, 1)])
                        P.wait("vector", cc.s_bc, cc.s_bc.count)
                        t_copy(P, "vector", rnB[:, tb * 512:(tb + 1) * 512], cc.pbc[:, 0:512],
                               inc=[(cc.s_bcev, 1)])
                    P.barrier()
                    for tb in range(TB):
                        def evr(e, pap, dst=pq[:, tb * 512:(tb + 1) * 512]):
                            return e.tensor_copy(dst, pap)
                        mx.tile(None, "vector", evr, [(prot[:, 0:128], buf[:, tb * 512:(tb + 1) * 512])])
                    P.barrier()
                    cs = csq if which == "q" else csk
                    ss = ssq if which == "q" else ssk
                    for tb in range(TB):
                        sl = slice(tb * 512, (tb + 1) * 512)
                        t_tensor(P, "vector", t1[:, sl], buf[:, sl], cs[:, sl], ALU.mult)
                        t_tensor(P, "gpsimd", pq[:, sl], pq[:, sl], ss[:, sl], ALU.mult)
                    P.barrier()
                    for tb in range(TB):
                        sl = slice(tb * 512, (tb + 1) * 512)
                        t_tensor(P, "vector" if tb % 2 == 0 else "gpsimd",
                                 t1[:, sl], t1[:, sl], pq[:, sl], ALU.add)
                    P.barrier()
                    for tb in range(TB):
                        sl = slice(tb * 512, (tb + 1) * 512)
                        t_tensor(P, "vector" if tb % 2 == 0 else "gpsimd",
                                 buf[:, sl], t1[:, sl], rnB[:, sl], ALU.mult)
                    P.barrier()
                for hh2 in range(2):
                    hloc = pair * 2 + hh2
                    hb = hh2 * 64
                    hog = hloc % 4
                    memset(P, "vector", vtok[:, :], 1.0)
                    P.barrier()
                    for st in range(16):
                        cc.transpose(vtok[:, st * 65: st * 65 + 64],
                                     v2[hb:hb + 64, st * 128:(st + 1) * 128], 64)
                    P.barrier()
                    for qt in range(16):
                        ks0 = max(0, qt * 128 - 256)
                        kw = qt * 128 + 128 - ks0
                        mx.wait_evacs("tensor")
                        mm(P, cc.paux[:, 0:kw], q2[hb:hb + 64, qt * 128:(qt + 1) * 128],
                           k2[hb:hb + 64, ks0:ks0 + kw], True, True, mmsem=s_mm)
                        P.wait("vector", s_mm, s_mm.count)
                        P.emit("vector", lambda e, o=mq16[:, qt:qt + 1], i=cc.paux[:, 0:kw]:
                               e.tensor_reduce(o, i, AX.X, ALU.max), inc=[(mx.evs["vector"], 1)])
                    P.barrier()
                    t_scalar(P, "vector", mq16[:, 0:16], mq16[:, 0:16], -1.0, ALU.mult)
                    P.barrier()
                    cc.transpose_f32(mqrow[0:16, 0:128], mq16[:, 0:16], 16)
                    P.barrier()
                    P.dma("sync", mqdram, mqrow[0:16, 0:128])
                    P.barrier()
                    P.dma("sync", mqrow[0:1, 0:2048], bass.AP(mqdram.tensor, 0, [[4096, 1], [1, 2048]]))
                    P.barrier()
                    t_copy(P, "vector", mqrowb[0:1, :], mqrow[0:1, 0:2048])
                    P.barrier()
                    for wdw in range(8):
                        q0 = wdw * 256
                        ks0 = (wdw - 1) * 256
                        vst = [stt for stt in range(4) if ks0 + stt * 128 >= 0]
                        for ii, stt in enumerate(vst):
                            kabs = ks0 + stt * 128
                            spb = mx.banks[stt % 2]
                            if sbank_ex2[stt % 2]:
                                P.wait("tensor", s_ex, sbank_ex2[stt % 2])
                            mm(P, spb[:, 0:256], k2[hb:hb + 64, kabs:kabs + 128],
                               q2[hb:hb + 64, q0:q0 + 256], True, False)
                            mm(P, spb[:, 0:256], onesrow_b[0:1, 0:128],
                               mqrowb[0:1, q0:q0 + 256], False, True, mmsem=s_mm)
                            P.wait("scalar", s_mm, s_mm.count)
                            if eslot_av2[0]:
                                P.wait("scalar", s_av, eslot_av2[0])
                            act(P, Ebuf[:, 0:256], spb[:, 0:256], AF.Exp, inc=[(s_ex, 1)])
                            sbank_ex2[stt % 2] = s_ex.count
                            P.wait("gpsimd", s_ex, s_ex.count)
                            P.emit("gpsimd", lambda e, o=Ebuf[:, 0:256], a=Ebuf[:, 0:256],
                                   b=amask[:, stt * 256: stt * 256 + 256]:
                                   e.tensor_tensor(o, a, b, ALU.mult), inc=[(s_dv, 1)])
                            P.wait("tensor", s_dv, s_dv.count)
                            ktile = kabs // 128
                            mm(P, cc.pav[0:65, 0:256], vtok[:, ktile * 65: ktile * 65 + 65],
                               Ebuf[:, 0:256], start=(ii == 0), stop=(ii == len(vst) - 1),
                               mmsem=s_av)
                            eslot_av2[0] = s_av.count
                        P.wait("vector", s_av, s_av.count)
                        t_scalar(P, "vector", dnm2[64:65, 0:256], cc.pav[64:65, 0:256], 1e-30, ALU.add)
                        P.barrier()
                        P.emit("vector", lambda e, o=dnm2[64:65, 0:256], i=dnm2[64:65, 0:256]: e.reciprocal(o, i))
                        act(P, obf[0:64, 0:256], cc.pav[0:64, 0:256], AF.Copy)
                        P.barrier()
                        P.emit("tensor", lambda e, o=cc.pbc[0:64, 0:256], l=ones64b[64:65, 0:64],
                               r=dnm2[64:65, 0:256]: e.matmul(o, l, r, start=True, stop=True), inc=[(s_ms, 1)])
                        P.wait("vector", s_ms, s_ms.count)
                        t_tensor(P, "vector", Obuf[0:64, hog * T + q0: hog * T + q0 + 256],
                                 obf[0:64, 0:256], cc.pbc[0:64, 0:256], ALU.mult)
                        P.barrier()
            WoTh = get_w("wo", li, grp)
            for o in range(DC):
                for tb in range(TB):
                    dst = x2[:, o * T + tb * 512: o * T + tb * 512 + 512]
                    def evo(e, pap, d=dst):
                        return e.tensor_tensor(d, d, pap, ALU.add)
                    mx.tile(None, "vector", evo,
                            [(WoTh[0:64, hh3 * 1024 + o * 128: hh3 * 1024 + o * 128 + 128],
                              Obuf[0:64, hh3 * T + tb * 512: hh3 * T + tb * 512 + 512]) for hh3 in range(4)])
            P.barrier()
            get_w("free_wo", li, grp)
        for b in (q2, k2, v2, pq, t1, vtok, Obuf, rnB, nrm, mqrow, mqrowb, csq, ssq, csk, ssk):
            ar.free_(b)
        shift_into(xs, x2)
        lng, lnb = get_w("ln", li)
        scr = {"sums": ar.alloc(T, F32, "lna_sums"), "sumsq": ar.alloc(T, F32, "lna_sumsq"),
               "m2": ar.alloc(T, F32, "lna_m2"), "mean": ar.alloc(T, F32, "lna_mean"),
               "rstd": ar.alloc(T, F32, "lna_rstd"), "meanB": ar.alloc(T, F32, "lna_meanB"),
               "rstdB": ar.alloc(T, F32, "lna_rstdB"), "sqb": ar.alloc(1024, BF16, "lna_sqb")}
        emit_ln_affine(P, cc, mx, xs, lng, lnb, scr)
        hbuf = ar.alloc(len(L2FC) * 512, BF16, "l2f_hbuf")
        upscr = ar.alloc(512, BF16, "l2f_ups")
        for t0 in range(0, T, 512):
            for hc, (h0, hn) in enumerate(L2FC):
                w1a, w1g, b1a, b1g = get_w("w1", li, hc)
                c0 = t0
                up_t = upscr
                def eva(e2, pap, dst=up_t[0:hn, 0:512], b=b1a, n=hn):
                    return e2.tensor_scalar(dst, pap, b[0:n, 0:1], None, ALU.add)
                mx.tile(None, "vector", eva,
                        [(w1a[:, k * 128: k * 128 + hn], xs[:, k * T + c0: k * T + c0 + 512]) for k in range(DC)],
                        out_rows=hn)
                def evg2(e2, pap, dst=hbuf[0:hn, hc * 512: hc * 512 + 512], b=b1g, n=hn):
                    return e2.activation(dst, pap, AF.Gelu, bias=b[0:n, 0:1], scale=1.0)
                mx.tile(None, "scalar", evg2,
                        [(w1g[:, k * 128: k * 128 + hn], xs[:, k * T + c0: k * T + c0 + 512]) for k in range(DC)],
                        out_rows=hn)
                mx.wait_evacs("vector")
                dst = hbuf[0:hn, hc * 512: hc * 512 + 512]
                t_tensor(P, "vector", dst, dst, up_t[0:hn, 0:512], ALU.mult)
                P.barrier()
            for dc in range(DC):
                w2t, b2c = get_w("w2", li, dc)
                c0 = t0
                def evd(e2, pap, dst=x2[:, dc * T + c0: dc * T + c0 + 512], b=b2c):
                    return e2.scalar_tensor_tensor(dst, pap, 1.0, dst, ALU.mult, ALU.add)
                pairs = [(w2t[0:hn2, kc * 128: kc * 128 + 128],
                          hbuf[0:hn2, kc * 512: kc * 512 + 512])
                         for kc, (h02, hn2) in enumerate(L2FC)]
                mx.tile(None, "vector", evd, pairs)
                mx.wait_evacs("vector")
                t_scalar(P, "vector", x2[:, dc * T + c0: dc * T + c0 + 512],
                         x2[:, dc * T + c0: dc * T + c0 + 512], b2c, ALU.add)
                P.barrier()
        for k2_ in scr.values():
            ar.free_(k2_)
        ar.free_(hbuf)
        ar.free_(upscr)
        P.barrier()
    finish_cb(x2)
    P.barrier()
    for b in (x2, xs, Ebuf, mq16, onesrow_b, dnm2, ones64b, obf):
        ar.free_(b)


def emit_ln_affine(P, cc, mx, xbuf, gcol, bcol, scr):
    sums, sumsq, m2, mean, rstd, meanB, rstdB, sqb = (scr[k] for k in
        ("sums", "sumsq", "m2", "mean", "rstd", "meanB", "rstdB", "sqb"))
    s_sq = P.sem("lna_sq")
    s_sqm = P.sem("lna_sqm")
    for tb in range(TB):
        sl = slice(tb * 512, (tb + 1) * 512)
        mx.wait_evacs("tensor")
        for dc in range(DC):
            mm(P, cc.paux[0:1, 0:512], cc.ones_col[:],
               xbuf[:, dc * T + tb * 512: dc * T + tb * 512 + 512],
               start=(dc == 0), stop=(dc == DC - 1), mmsem=mx.mms if dc == DC - 1 else None)
        P.wait("vector", mx.mms, mx.mms.count)
        t_copy(P, "vector", sums[0:1, sl], cc.paux[0:1, 0:512], inc=[(mx.evs["vector"], 1)])
        mx.wait_evacs("tensor")
        for dc in range(DC):
            half = (dc % 2) * 512
            if dc >= 2:
                P.wait("scalar", s_sqm, s_sqm.count - 1)
            act(P, sqb[:, half:half + 512], xbuf[:, dc * T + tb * 512: dc * T + tb * 512 + 512],
                AF.Square, inc=[(s_sq, 1)])
            P.wait("tensor", s_sq, s_sq.count)
            mm(P, cc.paux[0:1, 0:512], cc.ones_col[:], sqb[:, half:half + 512],
               start=(dc == 0), stop=(dc == DC - 1), mmsem=s_sqm)
        P.wait("vector", s_sqm, s_sqm.count)
        t_copy(P, "vector", sumsq[0:1, sl], cc.paux[0:1, 0:512], inc=[(mx.evs["vector"], 1)])
        mx.wait_evacs("tensor")
    P.barrier()
    t_scalar(P, "vector", mean[0:1, :], sums[0:1, :], 1.0 / D, ALU.mult)
    P.barrier()
    t_scalar(P, "vector", rstd[0:1, :], sumsq[0:1, :], 1.0 / D, ALU.mult)
    P.barrier()
    t_tensor(P, "vector", m2[0:1, :], mean[0:1, :], mean[0:1, :], ALU.mult)
    P.barrier()
    t_tensor(P, "vector", rstd[0:1, :], rstd[0:1, :], m2[0:1, :], ALU.subtract)
    P.barrier()
    t_scalar(P, "vector", rstd[0:1, :], rstd[0:1, :], 1e-5, ALU.add)
    P.barrier()
    act(P, rstd[0:1, :], rstd[0:1, :], AF.Sqrt)
    P.barrier()
    P.emit("vector", lambda e, o=rstd[0:1, :], i=rstd[0:1, :]: e.reciprocal(o, i))
    P.barrier()
    for tb in range(TB):
        sl = slice(tb * 512, (tb + 1) * 512)
        cc.bcast_rows(meanB[:, sl], mean[0:1, sl], 128)
        cc.bcast_rows(rstdB[:, sl], rstd[0:1, sl], 128)
    P.barrier()
    for dc in range(DC):
        eng = "vector" if dc % 2 == 0 else "gpsimd"
        t_tensor(P, eng, xbuf[:, dc * T:(dc + 1) * T], xbuf[:, dc * T:(dc + 1) * T], meanB[:], ALU.subtract)
    P.barrier()
    for dc in range(DC):
        eng = "vector" if dc % 2 == 0 else "gpsimd"
        t_tensor(P, eng, xbuf[:, dc * T:(dc + 1) * T], xbuf[:, dc * T:(dc + 1) * T], rstdB[:], ALU.mult)
    P.barrier()
    for dc in range(DC):
        eng = "vector" if dc % 2 == 0 else "gpsimd"
        t_scalar(P, eng, xbuf[:, dc * T:(dc + 1) * T], xbuf[:, dc * T:(dc + 1) * T],
                 gcol[:, dc:dc + 1], ALU.mult, s2=bcol[:, dc:dc + 1], op1=ALU.add)
    P.barrier()


def build_l2_ffn_test():
    P = Prog("l2f_test")
    nc = P.nc
    XN = nc.dram_tensor("XN", [D, T], F32, kind="ExternalInput").ap()
    WQKV = nc.dram_tensor("WQKV", [L2D, D, 3072], F32, kind="ExternalInput").ap()
    QSR = nc.dram_tensor("QSR", [L2D, 128, 1], F32, kind="ExternalInput").ap()
    KSR = nc.dram_tensor("KSR", [L2D, 128, 1], F32, kind="ExternalInput").ap()
    WOg = nc.dram_tensor("WOg", [L2D, 4, 64, 4 * 1024], F32, kind="ExternalInput").ap()
    LNG = nc.dram_tensor("LNG", [L2D, 128, DC], F32, kind="ExternalInput").ap()
    LNB = nc.dram_tensor("LNB", [L2D, 128, DC], F32, kind="ExternalInput").ap()
    W1 = nc.dram_tensor("W1", [L2D, D, 5460], F32, kind="ExternalInput").ap()
    B1 = nc.dram_tensor("B1", [L2D, 2, 128, 22], F32, kind="ExternalInput").ap()
    W2 = nc.dram_tensor("W2", [L2D, H1B, D], F32, kind="ExternalInput").ap()
    B2 = nc.dram_tensor("B2", [L2D, 128, DC], F32, kind="ExternalInput").ap()
    CSQ = nc.dram_tensor("CSQ", [128, T], F32, kind="ExternalInput").ap()
    SSQ = nc.dram_tensor("SSQ", [128, T], F32, kind="ExternalInput").ap()
    CSK = nc.dram_tensor("CSK", [128, T], F32, kind="ExternalInput").ap()
    SSK = nc.dram_tensor("SSK", [128, T], F32, kind="ExternalInput").ap()
    PROT = nc.dram_tensor("PROT", [128, 128], F32, kind="ExternalInput").ap()
    H2C = nc.dram_tensor("H2C", [128, 2], F32, kind="ExternalInput").ap()
    H2R = nc.dram_tensor("H2R", [2, 128], F32, kind="ExternalInput").ap()
    AMASK = nc.dram_tensor("AMASK", [128, 4 * 256], F32, kind="ExternalInput").ap()
    OUT = nc.dram_tensor("OUT", [D, T], BF16, kind="ExternalOutput").ap()
    mqd = nc.dram_tensor("l2f_mqd", [16, 128], F32, kind="Internal").ap()

    ar = Arena(P, 204 * 1024)
    xnb = ar.alloc(DC * T, BF16, "xnb")
    P.dma("gpsimd", xnb[:].rearrange("p (c t) -> p c t", c=DC), XN.rearrange("(c p) t -> p c t", p=128))
    P.barrier()
    mx = MatmulCtx(P, 4)
    cc = CommonCtx(P, mx)
    st = {}

    def ldc(nm, dr, rows, cols, dt=BF16, q="gpsimd"):
        if nm not in st:
            st[nm] = ar.alloc(cols, dt, nm)
            P.dma(q, st[nm][0:rows, :], dr)
            P.barrier()
        return st[nm]

    def get_w(which, *a):
        if which == "rotc":
            return {"csq": ldc("csq_c", CSQ, 128, T), "ssq": ldc("ssq_c", SSQ, 128, T),
                    "csk": ldc("csk_c", CSK, 128, T), "ssk": ldc("ssk_c", SSK, 128, T)}
        if which == "prot":
            return ldc("prot", PROT, 128, 128)
        if which == "h2col":
            return ldc("h2col", H2C, 128, 2, dt=BF16)
        if which == "h2row":
            return ldc("h2row", H2R, 2, 128, dt=F32, q="sync")
        if which == "amask":
            return ldc("amask", AMASK, 128, 4 * 256)
        if which == "mqdram":
            return mqd
        if which == "scales":
            li = a[0]
            if "qs" not in st:
                st["qs"] = ar.alloc(1, F32, "qs"); st["ks"] = ar.alloc(1, F32, "ks")
            P.barrier()
            P.dma("sync", st["qs"][:, :], QSR[li])
            P.dma("sync", st["ks"][:, :], KSR[li])
            P.barrier()
            return (st["qs"][:, 0:1], st["ks"][:, 0:1])
        if which == "qkv":
            li, pair = a
            if "wq" not in st:
                st["wq"] = ar.alloc(DC * 128, BF16, "wq")
                st["wk"] = ar.alloc(DC * 128, BF16, "wk")
                st["wv"] = ar.alloc(DC * 128, BF16, "wv")
            P.barrier()
            for nm, off in (("wq", 0), ("wk", 1024), ("wv", 2048)):
                P.dma("gpsimd", st[nm][:].rearrange("p (k c) -> p k c", k=DC),
                      WQKV[li].rearrange("(k p) n -> p k n", p=128)[:, :, off + pair * 128: off + pair * 128 + 128])
            P.barrier()
            return (st["wq"], st["wk"], st["wv"])
        if which == "wo":
            li, grp = a
            st["wo"] = ar.alloc(4 * 1024, BF16, "wo")
            P.dma("gpsimd", st["wo"][0:64, :], WOg[li, grp])
            P.barrier()
            return st["wo"]
        if which == "free_wo":
            ar.free_(st.pop("wo"))
            return None
        if which == "ln":
            li = a[0]
            if "lng" not in st:
                st["lng"] = ar.alloc(DC, F32, "lng"); st["lnb"] = ar.alloc(DC, F32, "lnb")
            P.barrier()
            P.dma("sync", st["lng"][:, :], LNG[li])
            P.dma("sync", st["lnb"][:, :], LNB[li])
            P.barrier()
            return (st["lng"], st["lnb"])
        if which == "w1":
            li, hc = a
            h0, hn = L2FC[hc]
            if "w1a" not in st:
                st["w1a"] = ar.alloc(DC * 128, BF16, "w1a")
                st["w1g"] = ar.alloc(DC * 128, BF16, "w1g")
                st["b1a"] = ar.alloc(22, F32, "b1a")
                st["b1g"] = ar.alloc(22, F32, "b1g")
                st["b1li"] = -1
            P.barrier()
            P.dma("gpsimd", st["w1a"][:].rearrange("p (k c) -> p k c", k=DC)[:, :, 0:hn],
                  W1[li].rearrange("(k p) n -> p k n", p=128)[:, :, h0:h0 + hn])
            P.dma("gpsimd", st["w1g"][:].rearrange("p (k c) -> p k c", k=DC)[:, :, 0:hn],
                  W1[li].rearrange("(k p) n -> p k n", p=128)[:, :, 2730 + h0:2730 + h0 + hn])
            if st["b1li"] != li:
                P.dma("sync", st["b1a"][:, :], B1[li, 0])
                P.dma("sync", st["b1g"][:, :], B1[li, 1])
                st["b1li"] = li
            P.barrier()
            return (st["w1a"], st["w1g"], st["b1a"][:, hc:hc + 1], st["b1g"][:, hc:hc + 1])
        if which == "w2":
            li, dc = a
            if "w2" not in st:
                st["w2"] = ar.alloc(22 * 128, BF16, "w2")
                st["b2"] = ar.alloc(DC, F32, "b2")
                st["b2li"] = -1
            P.barrier()
            P.dma("gpsimd", st["w2"][:].rearrange("p (k c) -> p k c", k=22)[:, 0:21, :],
                  W2[li, 0:2688].rearrange("(k p) c -> p k c", p=128)[:, :, dc * 128:(dc + 1) * 128])
            P.dma("gpsimd", st["w2"][0:42, :].rearrange("p (k c) -> p k c", k=22)[:, 21:22, :],
                  W2[li, 2688:2730].rearrange("q c -> q () c")[:, :, dc * 128:(dc + 1) * 128])
            if st["b2li"] != li:
                P.dma("sync", st["b2"][:, :], B2[li])
                st["b2li"] = li
            P.barrier()
            return (st["w2"], st["b2"][:, dc:dc + 1])
        raise KeyError(which)

    def finish(x2h):
        P.dma("sync", OUT.rearrange("(c p) t -> p c t", p=128), x2h[:].rearrange("p (c t) -> p c t", c=DC))
        P.barrier()

    emit_l2_ffn(P, cc, ar, xnb, finish, get_w, mx, free_xn=True)
    return P.finalize()



def emit_l0_ffn(P, cc, ar, xnfullT_dram, xn_bf, gT_out, get_w, mx, tbig=1024):
    lg = ar.alloc(1024, F32, "f0_lg")
    sc = ar.alloc(1024, F32, "f0_sc")
    scb = ar.alloc(1024, BF16, "f0_scb")
    msk = ar.alloc(1024, F32, "f0_msk")
    red = ar.alloc(16, F32, "f0_red")
    lo = ar.alloc(1, F32, "f0_lo")
    hi = ar.alloc(1, F32, "f0_hi")
    mid = ar.alloc(1, F32, "f0_mid")
    mcol = ar.alloc(1, F32, "f0_mcol")
    thr = ar.alloc(1, F32, "f0_thr")
    wGb = ar.alloc(2 * 1024, F32, "f0_wgb")
    wgrow = ar.alloc(2 * 1024, BF16, "f0_wgrow")
    s_ev = P.sem("f0_ev")
    esel = get_w("esel")
    gsel = get_w("gsel")
    oh128 = get_w("oh128")
    rwg = get_w("rwg")
    rbuf = ar.alloc(2 * 8 * 128, F32, "f0_rbuf")

    xsl = ar.alloc(DC * 512, F32, "f0_xsl")
    for halfc in range(2):
        mx.wait_evacs("tensor")
        for g in range(16):
            rb = rbuf[:, (g % 2) * 1024:(g % 2) * 1024 + 1024].rearrange("p (k c) -> p k c", k=DC)
            P.barrier()
            P.dma("sync", rb, rwg[g].rearrange("(k p) c -> p k c", p=128))
            P.dma("scalar", xsl[:].rearrange("p (k c) -> p k c", k=DC),
                  bass.AP(xnfullT_dram.tensor, g * 1024 + halfc * 512,
                          [[8 * T, 128], [128 * 8 * T, DC], [1, 512]]))
            P.barrier()
            for k in range(DC):
                mm(P, cc.paux[:, 0:512], rb[:, k, :], xsl[:, k * 512:(k + 1) * 512],
                   start=(g == 0 and k == 0), stop=(g == 15 and k == DC - 1),
                   mmsem=mx.mms if (g == 15 and k == DC - 1) else None)
        P.wait("vector", mx.mms, mx.mms.count)
        t_copy(P, "vector", lg[:, halfc * 512: halfc * 512 + 512], cc.paux[:, 0:512],
               inc=[(mx.evs["vector"], 1)])
    P.barrier()
    ar.free_(xsl)
    rbc = get_w("rbias")
    t_scalar(P, "vector", lg[:, :], lg[:, :], rbc[:, 0:1], ALU.add)
    P.barrier()
    act(P, sc[:, :], lg[:, :], AF.Exp)
    P.barrier()
    for halfc in range(2):
        sl = slice(halfc * 512, halfc * 512 + 512)
        mx.wait_evacs("tensor")
        mm(P, cc.paux[0:16, 0:512], gsel[:, 0:16], sc[:, sl], True, True, mmsem=mx.mms)
        P.wait("vector", mx.mms, mx.mms.count)
        t_copy(P, "vector", msk[0:16, sl], cc.paux[0:16, 0:512], inc=[(mx.evs["vector"], 1)])
        P.barrier()
        P.emit("vector", lambda e, o=msk[0:16, sl], i=msk[0:16, sl]: e.reciprocal(o, i))
        P.barrier()
        mx.wait_evacs("tensor")
        P.emit("tensor", lambda e, o=cc.paux[:, 0:512], l=get_w("gexp")[0:16, 0:128],
               r=msk[0:16, sl]: e.matmul(o, l, r, start=True, stop=True), inc=[(mx.mms, 1)])
        P.wait("vector", mx.mms, mx.mms.count)
        t_tensor(P, "vector", sc[:, sl], sc[:, sl], cc.paux[:, 0:512], ALU.mult,
                 inc=[(mx.evs["vector"], 1)])
    P.barrier()
    memset(P, "vector", lo[0:16, 0:1], 0.0)
    memset(P, "vector", hi[0:16, 0:1], 1.0)
    P.barrier()
    for it in range(34):
        t_tensor(P, "vector", mid[0:16, 0:1], lo[0:16, 0:1], hi[0:16, 0:1], ALU.add)
        P.barrier()
        t_scalar(P, "vector", mid[0:16, 0:1], mid[0:16, 0:1], 0.5, ALU.mult)
        P.barrier()
        mx.wait_evacs("tensor")
        P.emit("tensor", lambda e, o=cc.paux[:, 0:1], l=get_w("eexp")[0:16, 0:128],
               r=mid[0:16, 0:1]: e.matmul(o, l, r, start=True, stop=True), inc=[(mx.mms, 1)])
        P.wait("vector", mx.mms, mx.mms.count)
        t_copy(P, "vector", thr[:, 0:1], cc.paux[:, 0:1], inc=[(mx.evs["vector"], 1)])
        P.barrier()
        t_scalar(P, "vector", msk[:, :], sc[:, :], thr[:, 0:1], ALU.is_ge)
        P.barrier()
        reduce(P, "vector", mcol[:, 0:1], msk[:, :], ALU.add)
        P.barrier()
        mx.wait_evacs("tensor")
        P.emit("tensor", lambda e, o=cc.paux[0:16, 0:1], l=esel[:, 0:16],
               r=mcol[:, 0:1]: e.matmul(o, l, r, start=True, stop=True), inc=[(mx.mms, 1)])
        P.wait("vector", mx.mms, mx.mms.count)
        t_copy(P, "vector", red[0:16, 0:1], cc.paux[0:16, 0:1], inc=[(mx.evs["vector"], 1)])
        P.barrier()
        t_scalar(P, "vector", red[0:16, 0:1], red[0:16, 0:1], 2048.0, ALU.is_ge)
        P.barrier()
        t_tensor(P, "vector", mcol[0:16, 0:1], mid[0:16, 0:1], lo[0:16, 0:1], ALU.subtract)
        P.barrier()
        t_tensor(P, "vector", mcol[0:16, 0:1], mcol[0:16, 0:1], red[0:16, 0:1], ALU.mult)
        P.barrier()
        t_tensor(P, "vector", lo[0:16, 0:1], lo[0:16, 0:1], mcol[0:16, 0:1], ALU.add)
        P.barrier()
        t_scalar(P, "vector", red[0:16, 0:1], red[0:16, 0:1], -1.0, ALU.mult)
        P.barrier()
        t_scalar(P, "vector", red[0:16, 0:1], red[0:16, 0:1], 1.0, ALU.add)
        P.barrier()
        t_tensor(P, "vector", mcol[0:16, 0:1], mid[0:16, 0:1], hi[0:16, 0:1], ALU.subtract)
        P.barrier()
        t_tensor(P, "vector", mcol[0:16, 0:1], mcol[0:16, 0:1], red[0:16, 0:1], ALU.mult)
        P.barrier()
        t_tensor(P, "vector", hi[0:16, 0:1], hi[0:16, 0:1], mcol[0:16, 0:1], ALU.add)
        P.barrier()
    mx.wait_evacs("tensor")
    P.emit("tensor", lambda e, o=cc.paux[:, 0:1], l=get_w("eexp")[0:16, 0:128],
           r=lo[0:16, 0:1]: e.matmul(o, l, r, start=True, stop=True), inc=[(mx.mms, 1)])
    P.wait("vector", mx.mms, mx.mms.count)
    t_copy(P, "vector", thr[:, 0:1], cc.paux[:, 0:1], inc=[(mx.evs["vector"], 1)])
    P.barrier()
    t_scalar(P, "vector", msk[:, :], sc[:, :], thr[:, 0:1], ALU.is_ge)
    P.barrier()
    t_tensor(P, "vector", msk[:, :], msk[:, :], sc[:, :], ALU.mult)
    P.barrier()
    H0C = 32
    hbuf = ar.alloc(H0C * tbig, BF16, "f0_h")
    nt = tbig // 512
    for e in range(8):
        for g in range(2):
            js = e * 2 + g
            for c0 in (0, 512):
                mx.wait_evacs("tensor")
                P.emit("tensor", lambda e2, o=cc.paux[:, 0:512], l=oh128[:, js * 128:(js + 1) * 128],
                       r=msk[:, c0:c0 + 512]: e2.matmul(o, l, r, start=True, stop=True),
                       inc=[(mx.mms, 1)])
                P.wait("vector", mx.mms, mx.mms.count)
                t_copy(P, "vector", wGb[:, g * 1024 + c0: g * 1024 + c0 + 512], cc.paux[:, 0:512],
                       inc=[(mx.evs["vector"], 1)])
        P.barrier()
        t_copy(P, "vector", wgrow[:, :], wGb[:, :])
        P.barrier()
        for t0 in range(0, T, tbig):
            for hc in range(H0C):
                w1t, b1c = get_w("w1", e, hc)
                for tb in range(nt):
                    c0 = t0 + tb * 512
                    def evg(e2, pap, dst=hbuf[:, hc * tbig + tb * 512: hc * tbig + tb * 512 + 512],
                            b=b1c, wg=wgrow[:, c0:c0 + 512]):
                        return e2.activation(dst, pap, AF.Relu, bias=b, scale=1.0)
                    mx.tile(None, "scalar", evg,
                            [(w1t[:, k * 128:(k + 1) * 128], xn_bf[:, k * T + c0: k * T + c0 + 512])
                             for k in range(DC)])
            P.barrier()
            for hc in range(H0C):
                for tb in range(nt):
                    c0 = t0 + tb * 512
                    dst = hbuf[:, hc * tbig + tb * 512: hc * tbig + tb * 512 + 512]
                    t_tensor(P, "vector" if hc % 2 == 0 else "gpsimd", dst, dst,
                             wgrow[:, c0:c0 + 512], ALU.mult)
            P.barrier()
            for dc in range(DC):
                w2t, b2row = get_w("w2", e, dc)
                for tb in range(nt):
                    c0 = t0 + tb * 512
                    first = (e == 0)
                    def evd(e2, pap, dst=gT_out[:, dc * T + c0: dc * T + c0 + 512], f=first):
                        if f:
                            return e2.tensor_copy(dst, pap)
                        return e2.tensor_tensor(dst, dst, pap, ALU.add)
                    pairs = [(w2t[:, kc * 128:(kc + 1) * 128],
                              hbuf[:, kc * tbig + tb * 512: kc * tbig + tb * 512 + 512])
                             for kc in range(H0C)]
                    pairs.append((b2row[0:1, 0:128], wgrow[0:1, c0:c0 + 512]))
                    mx.tile(None, "vector", evd, pairs)
            P.barrier()
        get_w("free_e", e)
    for b in (lg, sc, scb, msk, red, lo, hi, mid, mcol, thr, wGb, wgrow, rbuf, hbuf):
        ar.free_(b)


def build_l0_ffn_test():
    P = Prog("l0f_test")
    nc = P.nc
    XNF = nc.dram_tensor("XNF", [D, 8 * T], F32, kind="ExternalInput").ap()
    RWG = nc.dram_tensor("RWG", [16, D, 128], F32, kind="ExternalInput").ap()
    ESEL = nc.dram_tensor("ESEL", [128, 16], F32, kind="ExternalInput").ap()
    GSEL = nc.dram_tensor("GSEL", [128, 16], F32, kind="ExternalInput").ap()
    GEXP = nc.dram_tensor("GEXP", [16, 128], F32, kind="ExternalInput").ap()
    EEXP = nc.dram_tensor("EEXP", [16, 128], F32, kind="ExternalInput").ap()
    OH128 = nc.dram_tensor("OH128", [128, 16 * 128], F32, kind="ExternalInput").ap()
    W1 = nc.dram_tensor("W1", [8, D, 4096], F32, kind="ExternalInput").ap()
    B1 = nc.dram_tensor("B1", [8, 128, 32], F32, kind="ExternalInput").ap()
    W2 = nc.dram_tensor("W2", [8, 4096, D], F32, kind="ExternalInput").ap()
    B2 = nc.dram_tensor("B2", [8, 1, D], F32, kind="ExternalInput").ap()
    RB = nc.dram_tensor("RB", [128, 1], F32, kind="ExternalInput").ap()
    OUT = nc.dram_tensor("OUT", [D, T], BF16, kind="ExternalOutput").ap()

    ar = Arena(P, 200 * 1024)
    xnb = ar.alloc(DC * T, BF16, "xnb")
    gT = ar.alloc(DC * T, BF16, "gT")
    consts = {}
    for nm, dr, rows, cols in (("esel", ESEL, 128, 16), ("gsel", GSEL, 128, 16),
                               ("gexp", GEXP, 16, 128), ("eexp", EEXP, 16, 128),
                               ("oh128", OH128, 128, 16 * 128)):
        consts[nm] = ar.alloc(cols, F32, nm)
        P.dma("sync", consts[nm][0:rows, :], dr)
    P.dma("gpsimd", xnb[:].rearrange("p (c t) -> p c t", c=DC),
          bass.AP(XNF.tensor, 0, [[8 * T, 128], [8 * T * 128, DC], [1, T]]))
    P.barrier()
    mx = MatmulCtx(P, 4)
    cc = CommonCtx(P, mx)
    st = {}

    def get_w(which, *a):
        if which in consts:
            return consts[which]
        if which == "rwg":
            return RWG
        if which == "w1":
            e, hc = a
            if "w1" not in st:
                st["w1"] = ar.alloc(DC * 128, BF16, "w1t")
                st["b1"] = ar.alloc(32, F32, "b1c")
                st["b1e"] = -1
            P.barrier()
            P.dma("gpsimd", st["w1"][:].rearrange("p (k c) -> p k c", k=DC),
                  W1[e].rearrange("(k p) h -> p k h", p=128)[:, :, hc * 128:(hc + 1) * 128])
            if st["b1e"] != e:
                P.dma("sync", st["b1"][:, :], B1[e])
                st["b1e"] = e
            P.barrier()
            return (st["w1"], st["b1"][:, hc:hc + 1])
        if which == "w2":
            e, dc = a
            if "w2" not in st:
                st["w2"] = ar.alloc(32 * 128, BF16, "w2t")
                st["b2"] = ar.alloc(128, BF16, "b2r")
            P.barrier()
            P.dma("gpsimd", st["w2"][:].rearrange("p (k c) -> p k c", k=32),
                  W2[e].rearrange("(k p) c -> p k c", p=128)[:, :, dc * 128:(dc + 1) * 128])
            P.dma("gpsimd", st["b2"][0:1, :], B2[e][:, dc * 128:(dc + 1) * 128])
            P.barrier()
            return (st["w2"], st["b2"])
        if which == "rbias":
            if "rb" not in st:
                st["rb"] = ar.alloc(1, F32, "rbc")
                P.dma("sync", st["rb"][:, :], RB)
                P.barrier()
            return st["rb"]
        if which == "free_e":
            return None

    emit_l0_ffn(P, cc, ar, XNF, xnb, gT, get_w, mx, tbig=1024)
    P.dma("sync", OUT.rearrange("(c p) t -> p c t", p=128), gT[:].rearrange("p (c t) -> p c t", c=DC))
    P.barrier()
    return P.finalize()



def emit_l1_diag(P, cc, ar, xn_bf, diag_out, w, mx):
    qp = ar.alloc(T, BF16, "l1qp")
    kp = ar.alloc(T, BF16, "l1kp")
    qk = ar.alloc(T, BF16, "l1qk")
    s_ev = P.sem("l1d_ev")
    for tb in range(TB):
        mx.wait_evacs("tensor")
        for ph in range(8):
            def evq(e, pap, dst=qp[:, tb * 512:tb * 512 + 512]):
                return e.activation(dst, pap, AF.Copy, bias=0.0, scale=1.0 / 32.0)
            mx.tile(None, "scalar", evq, [(w["WqTbd"][:, 0:128], xn_bf[:, ph * T + tb * 512: ph * T + tb * 512 + 512])])
            def evk(e, pap, dst=kp[:, tb * 512:tb * 512 + 512]):
                return e.tensor_copy(dst, pap)
            mx.tile(None, "vector", evk, [(w["WkTbd"][:, 0:128], xn_bf[:, ph * T + tb * 512: ph * T + tb * 512 + 512])])
            mx.wait_evacs("vector")
            t_tensor(P, "vector", qk[:, tb * 512:tb * 512 + 512], qp[:, tb * 512:tb * 512 + 512],
                     kp[:, tb * 512:tb * 512 + 512], ALU.mult, inc=[(s_ev, 1)])
            P.wait("tensor", s_ev, s_ev.count)
            mm(P, cc.paux[0:16, 0:512], w["hsel"][:, ph * 16: ph * 16 + 16],
               qk[:, tb * 512:tb * 512 + 512], start=(ph == 0), stop=(ph == 7),
               mmsem=mx.mms)
            P.wait("scalar", mx.mms, mx.mms.count)
            P.wait("vector", mx.mms, mx.mms.count)
        P.wait("vector", mx.mms, mx.mms.count)
        t_copy(P, "vector", diag_out[0:16, tb * 512:tb * 512 + 512], cc.paux[0:16, 0:512],
               inc=[(mx.evs['vector'], 1)])
    P.barrier()
    ar.free_(qp); ar.free_(kp); ar.free_(qk)


def emit_l1_rest(P, cc, ar, xn_bf, diag_all, sT_out, w, mx):
    ex = ar.alloc(T, BF16, "l1ex")
    den = ar.alloc(T, F32, "l1den")
    probs = ar.alloc(T, F32, "l1probs")
    vp = ar.alloc(T, BF16, "l1vp")
    vsum = ar.alloc(8, F32, "l1vsum")
    vsrow = ar.alloc(128, F32, "l1vsrow")
    s_ev = P.sem("l1r_ev")

    for ph in range(8):
        for tb in range(TB):
            def evv(e, pap, dst=vp[:, tb * 512:tb * 512 + 512]):
                return e.tensor_copy(dst, pap)
            mx.tile(None, "vector", evv, [(w["WvTbd"][:, 0:128], xn_bf[:, ph * T + tb * 512: ph * T + tb * 512 + 512])])
        mx.wait_evacs("vector")
        reduce(P, "vector", vsum[:, ph:ph + 1], vp[:, :], ALU.add, inc=[(s_ev, 1)])
    P.barrier()

    for tb in range(TB):
        sl = slice(tb * 512, tb * 512 + 512)
        act(P, ex[:, sl], diag_all[:, sl], AF.Exp)
        P.barrier()
        mm(P, cc.paux[0:16, 0:512], w["csel"][:, 0:16], ex[:, sl], True, True, mmsem=mx.mms)
        P.wait("vector", mx.mms, mx.mms.count)
        t_copy(P, "vector", den[0:16, sl], cc.paux[0:16, 0:512], inc=[(mx.evs['vector'], 1)])
        P.barrier()
        P.emit("vector", lambda e, o=den[0:16, sl], i=den[0:16, sl]: e.reciprocal(o, i))
        P.barrier()
        t_tensor(P, "vector", probs[0:16, sl], ex[0:16, sl], den[0:16, sl], ALU.mult)
        P.barrier()
        t_copy(P, "vector", ex[0:16, sl], probs[0:16, sl])
        P.barrier()
        for h in range(16):
            ph, hb = h // 2, (h % 2) * 64
            if cc.s_bcev.count:
                P.wait("tensor", cc.s_bcev, cc.s_bcev.count)
            P.emit("tensor", lambda e, o=cc.pbc[:, 0:512], l=w["oneh16"][0:16, h * 128:(h + 1) * 128],
                   r=ex[0:16, sl]: e.matmul(o, l, r, start=True, stop=True), inc=[(cc.s_bc, 1)])
            P.wait("vector", cc.s_bc, cc.s_bc.count)
            t_scalar(P, "vector", sT_out[hb:hb + 64, ph * T + tb * 512: ph * T + tb * 512 + 512],
                     cc.pbc[hb:hb + 64, 0:512], vsum[hb:hb + 64, ph:ph + 1], ALU.mult,
                     inc=[(cc.s_bcev, 1)])
    P.barrier()
    ar.free_(ex); ar.free_(den); ar.free_(probs); ar.free_(vp); ar.free_(vsum); ar.free_(vsrow)



H1 = 2730
H1C = [(i * 128, min(128, H1 - i * 128)) for i in range((H1 + 127) // 128)]


def emit_l1_ffn(P, cc, ar, xnT_f32, xn_bf, gT_out, get_w, mx, tbig=T, free_xnT=False):
    lgT = ar.alloc(T, F32, "l1f_lg")
    lgtok = ar.alloc(16 * 4, F32, "l1f_lgtok")
    m1 = ar.alloc(16, F32, "l1f_m1")
    m2 = ar.alloc(16, F32, "l1f_m2")
    wtok = ar.alloc(16 * 4, F32, "l1f_wtok")
    wTe = ar.alloc(T, F32, "l1f_wte")
    wBe = ar.alloc(T, F32, "l1f_wbe")
    s_ev = P.sem("l1f_ev")

    rw = get_w("router")
    for tb in range(TB):
        mx.wait_evacs("tensor")
        for k in range(DC):
            mm(P, cc.paux[0:4, 0:512], rw[:, k * 4:(k + 1) * 4],
               xnT_f32[:, k * T + tb * 512: k * T + tb * 512 + 512],
               start=(k == 0), stop=(k == DC - 1), mmsem=mx.mms if k == DC - 1 else None)
        P.wait("vector", mx.mms, mx.mms.count)
        t_copy(P, "vector", lgT[0:4, tb * 512: tb * 512 + 512], cc.paux[0:4, 0:512], inc=[(mx.evs['vector'], 1)])
    P.barrier()
    if free_xnT:
        ar.free_(xnT_f32)
    for tt in range(16):
        cc.transpose_f32(lgtok[:, tt * 4:(tt + 1) * 4], lgT[0:4, tt * 128:(tt + 1) * 128], 4)
    P.barrier()
    for tt in range(16):
        sl = slice(tt * 4, tt * 4 + 4)
        reduce(P, "vector", m1[:, tt:tt + 1], lgtok[:, sl], ALU.max)
        P.barrier()
        t_scalar(P, "vector", wtok[:, sl], lgtok[:, sl], m1[:, tt:tt + 1], ALU.is_equal)
        P.barrier()
        P.emit("vector", lambda e, o=wtok[:, sl], i=wtok[:, sl], s=NEG, i2=lgtok[:, sl]:
               e.scalar_tensor_tensor(o, i, s, i2, ALU.mult, ALU.add))
        P.barrier()
        reduce(P, "vector", m2[:, tt:tt + 1], wtok[:, sl], ALU.max)
        P.barrier()
        t_scalar(P, "vector", m1[:, tt:tt + 1], m1[:, tt:tt + 1], -1.0, ALU.mult)
        P.barrier()
        act(P, wtok[:, sl], lgtok[:, sl], AF.Exp, bias=m1[:, tt:tt + 1])
        P.barrier()
        t_scalar(P, "vector", lgtok[:, sl], lgtok[:, sl], m2[:, tt:tt + 1], ALU.is_ge)
        P.barrier()
        t_tensor(P, "vector", wtok[:, sl], wtok[:, sl], lgtok[:, sl], ALU.mult)
        P.barrier()
        reduce(P, "vector", m2[:, tt:tt + 1], wtok[:, sl], ALU.add)
        P.barrier()
        P.emit("vector", lambda e, o=m2[:, tt:tt + 1], i=m2[:, tt:tt + 1]: e.reciprocal(o, i))
        P.barrier()
        t_scalar(P, "vector", wtok[:, sl], wtok[:, sl], m2[:, tt:tt + 1], ALU.mult)
        P.barrier()
    for tt in range(16):
        cc.transpose_f32(wTe[0:4, tt * 128:(tt + 1) * 128], wtok[:, tt * 4:(tt + 1) * 4], 128)
    P.barrier()

    hbuf = ar.alloc(len(H1C) * tbig, BF16, "l1f_h")
    nt = tbig // 512
    for e in range(4):
        for tb in range(TB):
            if cc.s_bcev.count:
                P.wait("tensor", cc.s_bcev, cc.s_bcev.count)
            P.emit("tensor", lambda e2, o=cc.pbc[:, 0:512], l=get_w("onehot4")[0:4, e * 128:(e + 1) * 128],
                   r=wTe[0:4, tb * 512: tb * 512 + 512]: e2.matmul(o, l, r, start=True, stop=True),
                   inc=[(cc.s_bc, 1)])
            P.wait("vector", cc.s_bc, cc.s_bc.count)
            t_copy(P, "vector", wBe[:, tb * 512: tb * 512 + 512], cc.pbc[:, 0:512], inc=[(cc.s_bcev, 1)])
        P.barrier()
        for t0 in range(0, T, tbig):
            for hc, (h0, hn) in enumerate(H1C):
                gw, uw, gb, ub = get_w("gateup", e, hc)
                for tb in range(nt):
                    c0 = t0 + tb * 512
                    def evg(e2, pap, dst=hbuf[0:hn, hc * tbig + tb * 512: hc * tbig + tb * 512 + 512], b=gb, n=hn):
                        return e2.activation(dst, pap, AF.Silu, bias=b[0:n, 0:1], scale=1.0)
                    mx.tile(None, "scalar", evg,
                            [(gw[:, k * 128: k * 128 + hn], xn_bf[:, k * T + c0: k * T + c0 + 512]) for k in range(DC)],
                            out_rows=hn)
                    up_t = get_w("upscratch")
                    def evu2(e2, pap, dst=up_t[0:hn, 0:512], b=ub, n=hn):
                        return e2.tensor_scalar(dst, pap, b[0:n, 0:1], None, ALU.add)
                    mx.tile(None, "vector", evu2,
                            [(uw[:, k * 128: k * 128 + hn], xn_bf[:, k * T + c0: k * T + c0 + 512]) for k in range(DC)],
                            out_rows=hn)
                    mx.wait_evacs("vector")
                    dst = hbuf[0:hn, hc * tbig + tb * 512: hc * tbig + tb * 512 + 512]
                    t_tensor(P, "vector", dst, dst, up_t[0:hn, 0:512], ALU.mult)
                    P.barrier()
                    t_tensor(P, "vector", dst, dst, wBe[0:hn, c0:c0 + 512], ALU.mult)
                    P.barrier()
            for dc in range(DC):
                dw, dbrow = get_w("down", e, dc)
                for tb in range(nt):
                    c0 = t0 + tb * 512
                    first = (e == 0)
                    def evd(e2, pap, dst=gT_out[:, dc * T + c0: dc * T + c0 + 512], f=first):
                        if f:
                            return e2.tensor_copy(dst, pap)
                        return e2.tensor_tensor(dst, dst, pap, ALU.add)
                    wbbf = get_w("wbbf")
                    t_copy(P, "vector", wbbf[0:1, 0:512], wBe[0:1, c0:c0 + 512])
                    P.barrier()
                    pairs = [(dw[0:hn2, kc * 128: kc * 128 + 128],
                              hbuf[0:hn2, kc * tbig + tb * 512: kc * tbig + tb * 512 + 512])
                             for kc, (h02, hn2) in enumerate(H1C)]
                    pairs.append((dbrow[0:1, 0:128], wbbf[0:1, 0:512]))
                    mx.tile(None, "vector", evd, pairs)
            P.barrier()
        get_w("free_e")
    ar.free_(lgT); ar.free_(lgtok); ar.free_(m1); ar.free_(m2); ar.free_(wtok)
    ar.free_(wTe); ar.free_(wBe); ar.free_(hbuf)


def build_l1_attn_test():
    P = Prog("l1a_test")
    nc = P.nc
    XN = nc.dram_tensor("XN", [D, T], F32, kind="ExternalInput").ap()
    WQBD = nc.dram_tensor("WQBD", [128, 128], F32, kind="ExternalInput").ap()
    WKBD = nc.dram_tensor("WKBD", [128, 128], F32, kind="ExternalInput").ap()
    WVBD = nc.dram_tensor("WVBD", [128, 128], F32, kind="ExternalInput").ap()
    HSEL = nc.dram_tensor("HSEL", [128, 128], F32, kind="ExternalInput").ap()
    CSEL = nc.dram_tensor("CSEL", [128, 16], F32, kind="ExternalInput").ap()
    ONEH = nc.dram_tensor("ONEH", [16, 16 * 128], F32, kind="ExternalInput").ap()
    DIAGO = nc.dram_tensor("DIAGO", [112, T], F32, kind="ExternalInput").ap()
    OUT = nc.dram_tensor("OUT", [D, T], BF16, kind="ExternalOutput").ap()
    DOUT = nc.dram_tensor("DOUT", [16, T], F32, kind="ExternalOutput").ap()

    ar = Arena(P, 200 * 1024)
    xnb = ar.alloc(DC * T, BF16, "xnb")
    sT = ar.alloc(DC * T, BF16, "sT")
    diag_all = ar.alloc(T, F32, "diag_all")
    w = {}
    for nm, dr, cols, dt in (("WqTbd", WQBD, 128, BF16), ("WkTbd", WKBD, 128, BF16),
                             ("WvTbd", WVBD, 128, BF16), ("hsel", HSEL, 128, BF16),
                             ("csel", CSEL, 16, BF16), ("oneh16", ONEH, 16 * 128, BF16)):
        w[nm] = ar.alloc(cols, dt, nm)
        P.dma("gpsimd", w[nm][0:dr.shape[0], :], dr)
    P.dma("gpsimd", xnb[:].rearrange("p (c t) -> p c t", c=DC), XN.rearrange("(c p) t -> p c t", p=128))
    P.barrier()
    mx = MatmulCtx(P, 4)
    cc = CommonCtx(P, mx)
    emit_l1_diag(P, cc, ar, xnb, diag_all, w, mx)
    P.dma("sync", DOUT, diag_all[0:16, :])
    P.dma("sync", diag_all[16:128, :], DIAGO)
    P.barrier()
    emit_l1_rest(P, cc, ar, xnb, diag_all, sT, w, mx)
    P.dma("sync", OUT.rearrange("(c p) t -> p c t", p=128), sT[:].rearrange("p (c t) -> p c t", c=DC))
    P.barrier()
    return P.finalize()


def build_l1_ffn_test():
    P = Prog("l1f_test")
    nc = P.nc
    XN = nc.dram_tensor("XN", [D, T], F32, kind="ExternalInput").ap()
    RW = nc.dram_tensor("RW", [D, 4], F32, kind="ExternalInput").ap()
    GW = nc.dram_tensor("GW", [4, D, H1], F32, kind="ExternalInput").ap()
    UW = nc.dram_tensor("UW", [4, D, H1], F32, kind="ExternalInput").ap()
    DW = nc.dram_tensor("DW", [4, H1, D], F32, kind="ExternalInput").ap()
    GB = nc.dram_tensor("GB", [4, 128, 22], F32, kind="ExternalInput").ap()
    UB = nc.dram_tensor("UB", [4, 128, 22], F32, kind="ExternalInput").ap()
    DB = nc.dram_tensor("DB", [4, D], F32, kind="ExternalInput").ap()
    OH4 = nc.dram_tensor("OH4", [4, 512], F32, kind="ExternalInput").ap()
    OUT = nc.dram_tensor("OUT", [D, T], BF16, kind="ExternalOutput").ap()

    ar = Arena(P, 200 * 1024)
    xnT = ar.alloc(DC * T, F32, "xnT")
    xnb = ar.alloc(DC * T, BF16, "xnb")
    gT = ar.alloc(DC * T, BF16, "gT")
    P.dma("sync", xnT[:].rearrange("p (c t) -> p c t", c=DC), XN.rearrange("(c p) t -> p c t", p=128))
    P.dma("gpsimd", xnb[:].rearrange("p (c t) -> p c t", c=DC), XN.rearrange("(c p) t -> p c t", p=128))
    P.barrier()
    mx = MatmulCtx(P, 4)
    cc = CommonCtx(P, mx)
    st = {}

    def get_w(which, *a):
        if which == "router":
            if "rw" not in st:
                st["rw"] = ar.alloc(DC * 4, F32, "rw")
                P.dma("sync", st["rw"][:].rearrange("p (c n) -> p c n", c=DC),
                      RW.rearrange("(c p) n -> p c n", p=128))
                P.barrier()
            return st["rw"]
        if which == "onehot4":
            if "oh4" not in st:
                st["oh4"] = ar.alloc(512, F32, "oh4")
                P.dma("sync", st["oh4"][0:4, :], OH4)
                P.barrier()
            return st["oh4"]
        if which == "upscratch":
            if "ups" not in st:
                st["ups"] = ar.alloc(512, BF16, "ups")
            return st["ups"]
        if which == "wbbf":
            if "wbbf" not in st:
                st["wbbf"] = ar.alloc(512, BF16, "wbbf")
            return st["wbbf"]
        if which == "gateup":
            e, hc = a
            h0, hn = H1C[hc]
            if "gw" not in st:
                st["gw"] = ar.alloc(DC * 128, BF16, "gw")
                st["uw"] = ar.alloc(DC * 128, BF16, "uw")
                st["gb"] = ar.alloc(22, F32, "gb")
                st["ub"] = ar.alloc(22, F32, "ub")
                st["gb_e"] = -1
            P.barrier()
            P.dma("gpsimd", st["gw"][:].rearrange("p (k h) -> p k h", k=DC)[:, :, 0:hn],
                  GW[e].rearrange("(k p) h -> p k h", p=128)[:, :, h0:h0 + hn])
            P.dma("gpsimd", st["uw"][:].rearrange("p (k h) -> p k h", k=DC)[:, :, 0:hn],
                  UW[e].rearrange("(k p) h -> p k h", p=128)[:, :, h0:h0 + hn])
            if st["gb_e"] != e:
                P.dma("sync", st["gb"][:, :], GB[e])
                P.dma("sync", st["ub"][:, :], UB[e])
                st["gb_e"] = e
            P.barrier()
            return (st["gw"], st["uw"], st["gb"][:, hc:hc + 1], st["ub"][:, hc:hc + 1])
        if which == "down":
            e, dc = a
            if "dw" not in st:
                st["dw"] = ar.alloc(22 * 128, BF16, "dw")
                st["dbr"] = ar.alloc(128, BF16, "dbr")
            P.barrier()
            P.dma("gpsimd", st["dw"][:].rearrange("p (k c) -> p k c", k=22)[:, 0:21, :],
                  DW[e, 0:2688].rearrange("(k p) c -> p k c", p=128)[:, :, dc * 128:(dc + 1) * 128])
            P.dma("gpsimd", st["dw"][0:42, :].rearrange("p (k c) -> p k c", k=22)[:, 21:22, :],
                  DW[e, 2688:2730].rearrange("q c -> q () c")[:, :, dc * 128:(dc + 1) * 128])
            P.dma("gpsimd", st["dbr"][0:1, :], DB[e:e + 1, dc * 128:(dc + 1) * 128])
            P.barrier()
            return (st["dw"], st["dbr"])
        if which == "free_e":
            return None

    emit_l1_ffn(P, cc, ar, xnT, xnb, gT, get_w, mx, tbig=1024, free_xnT=True)
    P.dma("sync", OUT.rearrange("(c p) t -> p c t", p=128), gT[:].rearrange("p (c t) -> p c t", c=DC))
    P.barrier()
    return P.finalize()



def bf_perm():
    i = np.arange(4096)
    B, j = i // 128, i % 128
    return j * 32 + B


def host_stage_mats(up_weights):
    M = np.zeros((12, 4096, 4096), np.float32)
    for s in range(12):
        S = 1 << s
        k = np.arange(2048)
        ia = (k // S) * 2 * S + (k % S)
        ib = ia + S
        M[s, ia, ia] = up_weights[s, k, 0, 0]
        M[s, ia, ib] = up_weights[s, k, 0, 1]
        M[s, ib, ia] = up_weights[s, k, 1, 0]
        M[s, ib, ib] = up_weights[s, k, 1, 1]
    low = np.zeros((7, 32, 128, 128), np.float32)
    for s in range(7):
        for b in range(32):
            low[s, b] = M[s, b * 128:(b + 1) * 128, b * 128:(b + 1) * 128]
    perm = bf_perm()
    inv = np.empty(4096, np.int64)
    inv[perm] = np.arange(4096)
    high = np.zeros((5, 32, 128, 128), np.float32)
    for s in range(5):
        Mp = M[7 + s][np.ix_(inv, inv)]
        for g in range(32):
            high[s, g] = Mp[g * 128:(g + 1) * 128, g * 128:(g + 1) * 128]
    return low, high


def emit_bf_products(P, cc, ar, mx, stage_dram, nstages, dst):
    mbuf = ar.alloc(32 * 128, BF16, "bf_m")
    s_pr = P.sem("bf_pr")
    for k in range(nstages - 1, -1, -1):
        P.barrier()
        P.dma("gpsimd", mbuf[:].rearrange("p (b c) -> p b c", b=32), stage_dram[k].rearrange("b p c -> p b c"))
        P.barrier()
        for b in range(32):
            rhs = cc.ident[:] if k == nstages - 1 else dst[:, b * 128:(b + 1) * 128]
            def evp(e, pap, d=dst[:, b * 128:(b + 1) * 128]):
                return e.tensor_copy(d, pap)
            mx.tile(None, "vector", evp, [(mbuf[:, b * 128:(b + 1) * 128], rhs)], out_cols=128)
            mx.wait_evacs("tensor")
    P.barrier()
    ar.free_(mbuf)


def emit_l3_ffn(P, cc, ar, xn_bf, get_w, mx, combine_cb):
    uplow, uphigh, dnlow, dnhigh = get_w("bfmats")
    Wul = ar.alloc(32 * 128, BF16, "bf_ul")
    Wuh = ar.alloc(32 * 128, BF16, "bf_uh")
    Wdl = ar.alloc(32 * 128, BF16, "bf_dl")
    Wdh = ar.alloc(32 * 128, BF16, "bf_dh")
    emit_bf_products(P, cc, ar, mx, uplow, 7, Wul)
    emit_bf_products(P, cc, ar, mx, uphigh, 5, Wuh)
    emit_bf_products(P, cc, ar, mx, dnlow, 7, Wdl)
    emit_bf_products(P, cc, ar, mx, dnhigh, 5, Wdh)

    h1 = ar.alloc(32 * 512, BF16, "bf_h1")
    h2 = ar.alloc(32 * 512, BF16, "bf_h2")
    wbuf = ar.alloc(2 * 32 * 128, BF16, "bf_wbuf")
    bexp = get_w("bexp")
    bup = get_w("bup")
    bout = get_w("bout")
    pdram = get_w("pdram")
    wexp = get_w("wexp")
    wcon = get_w("wcon")

    def bounce(src_sbuf, dst_sbuf, inverse):
        P.barrier()
        if not inverse:
            P.dma("sync", pdram, src_sbuf[:, :])
            P.barrier()
            for r in range(4):
                sap = bass.AP(pdram.tensor, r * 16384, [[512, 32], [65536, 32], [1, 512]])
                dap = dst_sbuf[32 * r:32 * r + 32, :].rearrange("p (g t) -> p g t", g=32)
                P.dma("sync", dap, sap)
        else:
            for r in range(4):
                dap = bass.AP(pdram.tensor, r * 16384, [[512, 32], [65536, 32], [1, 512]])
                sap = src_sbuf[32 * r:32 * r + 32, :].rearrange("p (g t) -> p g t", g=32)
                P.dma("sync", dap, sap)
            P.barrier()
            P.dma("sync", dst_sbuf[:, :], pdram)
        P.barrier()

    for tb in range(TB):
        c0 = tb * 512
        for o in range(32):
            wb = wbuf[:, (o % 2) * 1024:(o % 2) * 1024 + 1024]
            P.barrier()
            P.dma("gpsimd", wb.rearrange("p (k c) -> p k c", k=DC),
                  wexp.rearrange("(k p) h -> p k h", p=128)[:, :, o * 128:(o + 1) * 128])
            P.barrier()
            def ev1(e, pap, dst=h1[:, o * 512:(o + 1) * 512], bc=bexp[:, o:o + 1]):
                return e.tensor_scalar(dst, pap, bc, None, ALU.add)
            mx.tile(None, "vector", ev1,
                    [(wb[:, k * 128:(k + 1) * 128], xn_bf[:, k * T + c0:k * T + c0 + 512]) for k in range(DC)])
        P.barrier()
        for b in range(32):
            def ev2(e, pap, dst=h2[:, b * 512:(b + 1) * 512]):
                return e.tensor_copy(dst, pap)
            mx.tile(None, "vector", ev2, [(Wul[:, b * 128:(b + 1) * 128], h1[:, b * 512:(b + 1) * 512])])
        bounce(h2, h1, inverse=False)
        for g in range(32):
            def ev3(e, pap, dst=h2[:, g * 512:(g + 1) * 512], bc=bup[:, g:g + 1]):
                return e.activation(dst, pap, AF.Gelu, bias=bc, scale=1.0)
            mx.tile(None, "scalar", ev3, [(Wuh[:, g * 128:(g + 1) * 128], h1[:, g * 512:(g + 1) * 512])])
        bounce(h2, h1, inverse=True)
        for b in range(32):
            def ev4(e, pap, dst=h2[:, b * 512:(b + 1) * 512]):
                return e.tensor_copy(dst, pap)
            mx.tile(None, "vector", ev4, [(Wdl[:, b * 128:(b + 1) * 128], h1[:, b * 512:(b + 1) * 512])])
        bounce(h2, h1, inverse=False)
        for g in range(32):
            def ev5(e, pap, dst=h2[:, g * 512:(g + 1) * 512]):
                return e.copy(dst, pap)
            mx.tile(None, "scalar", ev5, [(Wdh[:, g * 128:(g + 1) * 128], h1[:, g * 512:(g + 1) * 512])])
        P.barrier()
        for dc in range(DC):
            wb = wbuf[:, (dc % 2) * 4096:(dc % 2) * 4096 + 4096]
            P.barrier()
            P.dma("gpsimd", wb.rearrange("p (k c) -> p k c", k=32),
                  wcon.rearrange("(k p) d -> p k d", p=128)[:, :, dc * 128:(dc + 1) * 128])
            P.barrier()
            def ev6(e, pap, d=dc, t=tb):
                return combine_cb(e, d, t, pap)
            mx.tile(None, "vector", ev6,
                    [(wb[:, k * 128:(k + 1) * 128], h2[:, k * 512:(k + 1) * 512]) for k in range(32)])
            post = getattr(P, "_l3f_post", None)
            if post is not None:
                post(dc, tb)
        P.barrier()
    for bufh in (Wul, Wuh, Wdl, Wdh, h1, h2, wbuf):
        ar.free_(bufh)


def build_l3_ffn_test():
    P = Prog("l3f_test")
    nc = P.nc
    XN = nc.dram_tensor("XN", [D, T], F32, kind="ExternalInput").ap()
    UPL = nc.dram_tensor("UPL", [7, 32, 128, 128], F32, kind="ExternalInput").ap()
    UPH = nc.dram_tensor("UPH", [5, 32, 128, 128], F32, kind="ExternalInput").ap()
    DNL = nc.dram_tensor("DNL", [7, 32, 128, 128], F32, kind="ExternalInput").ap()
    DNH = nc.dram_tensor("DNH", [5, 32, 128, 128], F32, kind="ExternalInput").ap()
    WEXP = nc.dram_tensor("WEXP", [D, 4096], F32, kind="ExternalInput").ap()
    WCON = nc.dram_tensor("WCON", [4096, D], F32, kind="ExternalInput").ap()
    BEXP = nc.dram_tensor("BEXP", [128, 32], F32, kind="ExternalInput").ap()
    BUP = nc.dram_tensor("BUP", [128, 32], F32, kind="ExternalInput").ap()
    BOUT = nc.dram_tensor("BOUT", [128, 8], F32, kind="ExternalInput").ap()
    OUT = nc.dram_tensor("OUT", [D, T], BF16, kind="ExternalOutput").ap()
    pdram = nc.dram_tensor("l3f_pdram", [128, 16384], BF16, kind="Internal").ap()

    ar = Arena(P, 200 * 1024)
    xnb = ar.alloc(DC * T, BF16, "xnb")
    gT = ar.alloc(DC * T, BF16, "gT")
    bexp = ar.alloc(32, F32, "bexp")
    bup = ar.alloc(32, F32, "bup")
    bout = ar.alloc(8, F32, "bout")
    P.dma("gpsimd", xnb[:].rearrange("p (c t) -> p c t", c=DC), XN.rearrange("(c p) t -> p c t", p=128))
    P.dma("sync", bexp[:, :], BEXP)
    P.dma("sync", bup[:, :], BUP)
    P.dma("sync", bout[:, :], BOUT)
    P.barrier()
    mx = MatmulCtx(P, 4)
    cc = CommonCtx(P, mx)

    def get_w(which):
        return {"bfmats": (UPL, UPH, DNL, DNH), "wexp": WEXP, "wcon": WCON,
                "bexp": bexp, "bup": bup, "bout": bout, "pdram": pdram}[which]

    def combine_cb(e, dc, tb, pap):
        return e.tensor_scalar(gT[:, dc * T + tb * 512: dc * T + tb * 512 + 512],
                               pap, bout[:, dc:dc + 1], None, ALU.add)

    emit_l3_ffn(P, cc, ar, xnb, get_w, mx, combine_cb)
    P.dma("sync", OUT.rearrange("(c p) t -> p c t", p=128), gT[:].rearrange("p (c t) -> p c t", c=DC))
    P.barrier()
    return P.finalize()



def emit_l2_attn(P, xnT_bf, sT_out, w, mx, bufs=None):
    nc = P.nc
    if bufs is None:
        u = P.sbuf("l2a_u", [64, T], F32)
        hsc = P.sbuf("l2a_h", [64, T], F32)
        hbf = P.sbuf("l2a_hbf", [64, T], BF16)
        atile = P.sbuf("l2a_atile", [64, T], F32)
        WoutC_ = P.sbuf("l2a_woutc", [64, 1024], BF16)
    else:
        u, hsc, hbf, atile, WoutC_ = (bufs[k] for k in ("u", "hsc", "hbf", "atile", "WoutC"))
    dt1 = P.sbuf("l2a_dt", [1, 1], F32)
    dtB = P.sbuf("l2a_dtB", [64, 1], F32)
    Acol = P.sbuf("l2a_A", [64, 1], F32)
    Abar = P.sbuf("l2a_Abar", [64, 1], F32)
    frac = P.sbuf("l2a_frac", [64, 1], F32)
    Bbar = P.sbuf("l2a_Bbar", [64, 1], F32)
    mk = P.sbuf("l2a_mask", [64, 1], F32)
    ones64 = P.sbuf("l2a_ones", [1, 64], BF16)
    WoutC = WoutC_

    act(P, dt1[:], w["logdt"][:], AF.Exp)
    memset(P, "vector", ones64[:], 1.0)
    P.barrier()
    pb = mx.banks[0]
    dt1b = P.sbuf("l2a_dt1b", [1, 1], BF16)
    t_copy(P, "vector", dt1b[:], dt1[:])
    P.barrier()
    mm(P, pb[0:64, 0:1], ones64[0:1, 0:64], dt1b[:], True, True)
    P.barrier()
    t_copy(P, "vector", dtB[:], pb[0:64, 0:1])
    act(P, Acol[:], w["logA"][:], AF.Exp)
    P.barrier()
    t_scalar(P, "vector", Acol[:], Acol[:], -1.0, ALU.mult)
    P.barrier()
    t_tensor(P, "vector", Abar[:], Acol[:], dtB[:], ALU.mult)
    P.barrier()
    act(P, Abar[:], Abar[:], AF.Exp)
    P.barrier()
    t_scalar(P, "vector", frac[:], Abar[:], -1.0, ALU.add)
    P.barrier()
    rA = P.sbuf("l2a_rA", [64, 1], F32)
    P.emit("vector", lambda e, o=rA[:], i=Acol[:]: e.reciprocal(o, i))
    P.barrier()
    t_tensor(P, "vector", frac[:], frac[:], rA[:], ALU.mult)
    t_scalar(P, "vector", mk[:], Acol[:], -1.0, ALU.mult)
    P.barrier()
    t_scalar(P, "vector", mk[:], mk[:], 1e-6, ALU.is_lt)
    P.barrier()
    dmf = P.sbuf("l2a_dmf", [64, 1], F32)
    t_tensor(P, "vector", dmf[:], dtB[:], frac[:], ALU.subtract)
    P.barrier()
    t_tensor(P, "vector", dmf[:], dmf[:], mk[:], ALU.mult)
    P.barrier()
    t_tensor(P, "vector", frac[:], frac[:], dmf[:], ALU.add)
    P.barrier()
    t_tensor(P, "vector", Bbar[:], frac[:], w["Bp"][:], ALU.mult)
    t_scalar(P, "vector", WoutC[:], w["WoutT"][:], w["Cp"][:], ALU.mult)
    memset(P, "vector", atile[:], 1.0)
    P.barrier()
    t_scalar(P, "vector", atile[:], atile[:], Abar[:], ALU.mult)
    P.barrier()

    for tb in range(TB):
        def ev(e, pap, dst=u[:, tb * 512:(tb + 1) * 512], bb=Bbar[:]):
            return e.tensor_scalar(dst, pap, bb, None, ALU.mult)
        mx.tile(None, "vector", ev,
                [(w["WinT"][:, k * 64:(k + 1) * 64],
                  xnT_bf[:, k * T + tb * 512: k * T + tb * 512 + 512]) for k in range(DC)],
                out_rows=64)
    P.barrier()

    P.emit("vector", lambda e, o=hsc[:], a=atile[:], b=u[:]: e.tensor_tensor_scan(o, a, b, 0.0, ALU.mult, ALU.add))
    P.barrier()
    t_copy(P, "vector", hbf[:], hsc[:])
    P.barrier()

    for dc in range(DC):
        for tb in range(TB):
            def ev2(e, pap, dst=sT_out[:, dc * T + tb * 512: dc * T + tb * 512 + 512]):
                return e.tensor_copy(dst, pap)
            mx.tile(None, "vector", ev2,
                    [(WoutC[:, dc * 128:(dc + 1) * 128], hbf[:, tb * 512:(tb + 1) * 512])])
    P.barrier()
    return {"u": u, "h": hsc, "Abar": Abar, "Bbar": Bbar, "atile": atile,
            "mk": mk, "frac": frac, "rA": rA, "dtB": dtB}



def build_l2_attn_test():
    P = Prog("l2a_test")
    nc = P.nc
    XN = nc.dram_tensor("XN", [D, T], F32, kind="ExternalInput").ap()
    WINT = nc.dram_tensor("WINT", [D, 64], F32, kind="ExternalInput").ap()
    WOUTT = nc.dram_tensor("WOUTT", [64, D], F32, kind="ExternalInput").ap()
    LOGA = nc.dram_tensor("LOGA", [64, 1], F32, kind="ExternalInput").ap()
    BP = nc.dram_tensor("BP", [64, 1], F32, kind="ExternalInput").ap()
    CP = nc.dram_tensor("CP", [64, 1], F32, kind="ExternalInput").ap()
    LOGDT = nc.dram_tensor("LOGDT", [1, 1], F32, kind="ExternalInput").ap()
    OUT = nc.dram_tensor("OUT", [D, T], F32, kind="ExternalOutput").ap()
    DBG_U = nc.dram_tensor("DBG_U", [64, T], F32, kind="ExternalOutput").ap()
    DBG_H = nc.dram_tensor("DBG_H", [64, T], F32, kind="ExternalOutput").ap()
    DBG_AB = nc.dram_tensor("DBG_AB", [64, 6], F32, kind="ExternalOutput").ap()

    xnT = P.sbuf("xnT", [128, DC * T], BF16)
    sT = P.sbuf("sT", [128, DC * T], BF16)
    sTf = P.sbuf("sTf", [128, DC * T], F32)
    w = {
        "WinT": P.sbuf("w_wint", [128, DC * 64], BF16),
        "WoutT": P.sbuf("w_woutt", [64, 1024], F32),
        "logA": P.sbuf("w_loga", [64, 1], F32),
        "Bp": P.sbuf("w_bp", [64, 1], F32),
        "Cp": P.sbuf("w_cp", [64, 1], F32),
        "logdt": P.sbuf("w_logdt", [1, 1], F32),
    }
    P.dma("gpsimd", xnT[:].rearrange("p (c t) -> p c t", c=DC), XN.rearrange("(c p) t -> p c t", p=128))
    P.dma("gpsimd", w["WinT"][:].rearrange("p (c n) -> p c n", c=DC), WINT.rearrange("(c p) n -> p c n", p=128))
    P.dma("sync", w["WoutT"][:], WOUTT)
    P.dma("sync", w["logA"][:], LOGA)
    P.dma("sync", w["Bp"][:], BP)
    P.dma("sync", w["Cp"][:], CP)
    P.dma("sync", w["logdt"][:], LOGDT)
    P.barrier()

    mx = MatmulCtx(P, 4)
    dbg = emit_l2_attn(P, xnT, sT, w, mx)
    t_copy(P, "vector", sTf[:], sT[:])
    P.barrier()
    P.dma("sync", OUT.rearrange("(c p) t -> p c t", p=128), sTf[:].rearrange("p (c t) -> p c t", c=DC))
    P.dma("sync", DBG_U, dbg["u"][:])
    P.dma("sync", DBG_H, dbg["h"][:])
    P.dma("sync", DBG_AB[:, 0:1], dbg["Abar"][:], allow_slow_non_contiguous=True)
    P.dma("sync", DBG_AB[:, 1:2], dbg["Bbar"][:], allow_slow_non_contiguous=True)
    P.dma("sync", DBG_AB[:, 2:3], dbg["mk"][:], allow_slow_non_contiguous=True)
    P.dma("sync", DBG_AB[:, 3:4], dbg["frac"][:], allow_slow_non_contiguous=True)
    P.dma("sync", DBG_AB[:, 4:5], dbg["rA"][:], allow_slow_non_contiguous=True)
    P.dma("sync", DBG_AB[:, 5:6], dbg["dtB"][:], allow_slow_non_contiguous=True)
    P.barrier()
    return P.finalize()



def build_launch1():
    P = Prog("launch1")
    nc = P.nc
    X = nc.dram_tensor("X", [D, T], F32, kind="ExternalInput").ap()
    WQT = nc.dram_tensor("WQT", [D, D], F32, kind="ExternalInput").ap()
    WKT = nc.dram_tensor("WKT", [D, D], F32, kind="ExternalInput").ap()
    WVT = nc.dram_tensor("WVT", [D, D], F32, kind="ExternalInput").ap()
    WCTH = nc.dram_tensor("WCTH", [64, 16 * 1024], F32, kind="ExternalInput").ap()
    DEXP = nc.dram_tensor("DEXP", [128, 16], F32, kind="ExternalInput").ap()
    MDREP = nc.dram_tensor("MDREP", [32, 16], F32, kind="ExternalInput").ap()
    EMATT = nc.dram_tensor("EMATT", [32, 32], F32, kind="ExternalInput").ap()
    MASKT = nc.dram_tensor("MASKT", [32, 32], F32, kind="ExternalInput").ap()
    TMROW = nc.dram_tensor("TMROW", [1, 64], F32, kind="ExternalInput").ap()
    XNOUT = nc.dram_tensor("XNOUT", [D, T], F32, kind="ExternalOutput").ap()
    SOUT = nc.dram_tensor("SOUT", [D, T], BF16, kind="ExternalOutput").ap()

    ar = Arena(P, 204 * 1024)
    mx = MatmulCtx(P, 4)
    cc = CommonCtx(P, mx)
    xbuf = ar.alloc(DC * T, F32, "xbuf")
    P.dma("sync", xbuf[:].rearrange("p (c t) -> p c t", c=DC), X.rearrange("(c p) t -> p c t", p=128))
    cst = {"decayexp_src": ar.alloc(16, F32, "c_dexp"), "mdrep": ar.alloc(16, F32, "c_md"),
           "ematT": ar.alloc(32, F32, "c_em"), "maskT": ar.alloc(32, F32, "c_mk"),
           "tmrow": ar.alloc(64, F32, "c_tm")}
    P.dma("sync", cst["decayexp_src"][:, :], DEXP)
    P.dma("sync", cst["mdrep"][0:32, :], MDREP)
    P.dma("sync", cst["ematT"][0:32, :], EMATT)
    P.dma("sync", cst["maskT"][0:32, :], MASKT)
    P.dma("sync", cst["tmrow"][0:1, 0:64], TMROW)
    P.barrier()
    xnb = ar.alloc(DC * T, BF16, "xnb")
    scr = ln_scratch_ar(ar)
    emit_ln(P, cc, xbuf, xbuf, xnb, scr)
    ln_scratch_free(ar, scr)
    P.dma("sync", XNOUT.rearrange("(c p) t -> p c t", p=128), xbuf[:].rearrange("p (c t) -> p c t", c=DC))
    P.barrier()
    ar.free_(xbuf)
    sT = ar.alloc(DC * T, BF16, "sT")
    state = {}

    def get_w(phase, half=None):
        if phase == "consts":
            return cst
        if phase == "qkv":
            state["WqTh"] = ar.alloc(DC * 256, BF16, "wqth")
            state["WkTh"] = ar.alloc(DC * 256, BF16, "wkth")
            state["WvTh"] = ar.alloc(DC * 256, BF16, "wvth")
            for nm, dr in (("WqTh", WQT), ("WkTh", WKT), ("WvTh", WVT)):
                P.dma("gpsimd", state[nm][:].rearrange("p (c n) -> p c n", c=DC),
                      dr.rearrange("(c p) n -> p c n", p=128)[:, :, half * 256:half * 256 + 256])
            P.barrier()
            return state
        if phase == "free_qkv":
            ar.free_(state["WqTh"]); ar.free_(state["WkTh"]); ar.free_(state["WvTh"])
            return None
        if phase == "wc":
            wcth = ar.alloc(4 * 1024, BF16, "wcth")
            state["wcth"] = wcth
            P.dma("gpsimd", wcth[0:64, :], WCTH[:, half * 4096:half * 4096 + 4096])
            P.barrier()
            return wcth
        if phase == "free_wc":
            ar.free_(state["wcth"])
            return None

    emit_l0_attn(P, cc, ar, xnb, sT, get_w, mx)
    P.dma("sync", SOUT.rearrange("(c p) t -> p c t", p=128), sT[:].rearrange("p (c t) -> p c t", c=DC))
    P.barrier()
    return P.finalize()


def build_launch2():
    P = Prog("launch2")
    nc = P.nc
    XNF = nc.dram_tensor("XNF", [D, 8 * T], F32, kind="ExternalInput").ap()
    RWG = nc.dram_tensor("RWG", [16, D, 128], F32, kind="ExternalInput").ap()
    ESEL = nc.dram_tensor("ESEL", [128, 16], F32, kind="ExternalInput").ap()
    GSEL = nc.dram_tensor("GSEL", [128, 16], F32, kind="ExternalInput").ap()
    GEXP = nc.dram_tensor("GEXP", [16, 128], F32, kind="ExternalInput").ap()
    EEXP = nc.dram_tensor("EEXP", [16, 128], F32, kind="ExternalInput").ap()
    OH128 = nc.dram_tensor("OH128", [128, 16 * 128], F32, kind="ExternalInput").ap()
    W1 = nc.dram_tensor("W1", [8, D, 4096], F32, kind="ExternalInput").ap()
    B1 = nc.dram_tensor("B1", [8, 128, 32], F32, kind="ExternalInput").ap()
    W2 = nc.dram_tensor("W2", [8, 4096, D], F32, kind="ExternalInput").ap()
    B2 = nc.dram_tensor("B2", [8, 1, D], F32, kind="ExternalInput").ap()
    RB = nc.dram_tensor("RB", [128, 1], F32, kind="ExternalInput").ap()
    G0 = nc.dram_tensor("G0", [D, T], BF16, kind="ExternalOutput").ap()

    ar = Arena(P, 204 * 1024)
    xnb = ar.alloc(DC * T, BF16, "xnb")
    gT = ar.alloc(DC * T, BF16, "gT")
    consts = {}
    for nm, dr, rows, cols in (("esel", ESEL, 128, 16), ("gsel", GSEL, 128, 16),
                               ("gexp", GEXP, 16, 128), ("eexp", EEXP, 16, 128),
                               ("oh128", OH128, 128, 16 * 128)):
        consts[nm] = ar.alloc(cols, F32, nm)
        P.dma("sync", consts[nm][0:rows, :], dr)
    P.dma("gpsimd", xnb[:].rearrange("p (c t) -> p c t", c=DC),
          bass.AP(XNF.tensor, 0, [[8 * T, 128], [8 * T * 128, DC], [1, T]]))
    P.barrier()
    mx = MatmulCtx(P, 4)
    cc = CommonCtx(P, mx)
    st = {}

    def get_w(which, *a):
        if which in consts:
            return consts[which]
        if which == "rwg":
            return RWG
        if which == "w1":
            e, hc = a
            if "w1" not in st:
                st["w1"] = ar.alloc(DC * 128, BF16, "w1t")
                st["b1"] = ar.alloc(32, F32, "b1c")
                st["b1e"] = -1
            P.barrier()
            P.dma("gpsimd", st["w1"][:].rearrange("p (k c) -> p k c", k=DC),
                  W1[e].rearrange("(k p) h -> p k h", p=128)[:, :, hc * 128:(hc + 1) * 128])
            if st["b1e"] != e:
                P.dma("sync", st["b1"][:, :], B1[e])
                st["b1e"] = e
            P.barrier()
            return (st["w1"], st["b1"][:, hc:hc + 1])
        if which == "w2":
            e, dc = a
            if "w2" not in st:
                st["w2"] = ar.alloc(32 * 128, BF16, "w2t")
                st["dbr"] = ar.alloc(128, BF16, "dbr")
            P.barrier()
            P.dma("gpsimd", st["w2"][:].rearrange("p (k c) -> p k c", k=32),
                  W2[e].rearrange("(k p) c -> p k c", p=128)[:, :, dc * 128:(dc + 1) * 128])
            P.dma("gpsimd", st["dbr"][0:1, :], B2[e][:, dc * 128:(dc + 1) * 128])
            P.barrier()
            return (st["w2"], st["dbr"])
        if which == "rbias":
            if "rb" not in st:
                st["rb"] = ar.alloc(1, F32, "rbc")
                P.dma("sync", st["rb"][:, :], RB)
                P.barrier()
            return st["rb"]
        if which == "free_e":
            return None

    emit_l0_ffn(P, cc, ar, XNF, xnb, gT, get_w, mx, tbig=1024)
    P.dma("sync", G0.rearrange("(c p) t -> p c t", p=128), gT[:].rearrange("p (c t) -> p c t", c=DC))
    P.barrier()
    return P.finalize()


def emit_combine(P, cc, ar, xbuf, sbuf_, gbuf):
    for dc in range(DC):
        act(P, gbuf[:, dc * T:(dc + 1) * T], gbuf[:, dc * T:(dc + 1) * T], AF.Silu)
    P.barrier()
    for dc in range(DC):
        eng = "vector" if dc % 2 == 0 else "gpsimd"
        t_tensor(P, eng, gbuf[:, dc * T:(dc + 1) * T], gbuf[:, dc * T:(dc + 1) * T],
                 sbuf_[:, dc * T:(dc + 1) * T], ALU.mult)
    P.barrier()
    for dc in range(DC):
        eng = "vector" if dc % 2 == 0 else "gpsimd"
        t_tensor(P, eng, xbuf[:, dc * T:(dc + 1) * T], xbuf[:, dc * T:(dc + 1) * T],
                 gbuf[:, dc * T:(dc + 1) * T], ALU.add)
    P.barrier()


def build_launch3():
    P = Prog("launch3")
    nc = P.nc
    X = nc.dram_tensor("X", [D, T], F32, kind="ExternalInput").ap()
    S0 = nc.dram_tensor("S0", [D, T], BF16, kind="ExternalInput").ap()
    G0 = nc.dram_tensor("G0", [D, T], BF16, kind="ExternalInput").ap()
    WQBD = nc.dram_tensor("WQBD", [128, 128], F32, kind="ExternalInput").ap()
    WKBD = nc.dram_tensor("WKBD", [128, 128], F32, kind="ExternalInput").ap()
    HSEL = nc.dram_tensor("HSEL", [128, 128], F32, kind="ExternalInput").ap()
    X1OUT = nc.dram_tensor("X1OUT", [D, T], F32, kind="ExternalOutput").ap()
    XN1OUT = nc.dram_tensor("XN1OUT", [D, T], F32, kind="ExternalOutput").ap()
    DIAG = nc.dram_tensor("DIAG", [16, T], F32, kind="ExternalOutput").ap()

    ar = Arena(P, 204 * 1024)
    mx = MatmulCtx(P, 4)
    cc = CommonCtx(P, mx)
    xbuf = ar.alloc(DC * T, F32, "xbuf")
    sbuf_ = ar.alloc(DC * T, BF16, "sbuf")
    gbuf = ar.alloc(DC * T, BF16, "gbuf")
    P.dma("sync", xbuf[:].rearrange("p (c t) -> p c t", c=DC), X.rearrange("(c p) t -> p c t", p=128))
    P.dma("scalar", sbuf_[:].rearrange("p (c t) -> p c t", c=DC), S0.rearrange("(c p) t -> p c t", p=128))
    P.dma("gpsimd", gbuf[:].rearrange("p (c t) -> p c t", c=DC), G0.rearrange("(c p) t -> p c t", p=128))
    w = {}
    for nm, dr in (("WqTbd", WQBD), ("WkTbd", WKBD), ("hsel", HSEL)):
        w[nm] = ar.alloc(128, BF16, nm)
        P.dma("gpsimd", w[nm][:, :], dr)
    P.barrier()
    emit_combine(P, cc, ar, xbuf, sbuf_, gbuf)
    P.dma("sync", X1OUT.rearrange("(c p) t -> p c t", p=128), xbuf[:].rearrange("p (c t) -> p c t", c=DC))
    ar.free_(sbuf_); ar.free_(gbuf)
    xnb = ar.alloc(DC * T, BF16, "xnb")
    scr = ln_scratch_ar(ar)
    emit_ln(P, cc, xbuf, xbuf, xnb, scr)
    ln_scratch_free(ar, scr)
    P.dma("sync", XN1OUT.rearrange("(c p) t -> p c t", p=128), xbuf[:].rearrange("p (c t) -> p c t", c=DC))
    P.barrier()
    ar.free_(xbuf)
    diag = ar.alloc(T, F32, "diag")
    emit_l1_diag(P, cc, ar, xnb, diag, w, mx)
    P.dma("sync", DIAG, diag[0:16, :])
    P.barrier()
    return P.finalize()



def build_launch4a():
    P = Prog("launch4a")
    nc = P.nc
    X1 = nc.dram_tensor("X1", [D, T], F32, kind="ExternalInput").ap()
    XN1 = nc.dram_tensor("XN1", [D, T], F32, kind="ExternalInput").ap()
    DIAGA = nc.dram_tensor("DIAGA", [128, T], F32, kind="ExternalInput").ap()
    WVBD = nc.dram_tensor("WVBD", [128, 128], F32, kind="ExternalInput").ap()
    CSEL = nc.dram_tensor("CSEL", [128, 16], F32, kind="ExternalInput").ap()
    ONEH = nc.dram_tensor("ONEH", [16, 16 * 128], F32, kind="ExternalInput").ap()
    RW1 = nc.dram_tensor("RW1", [D, 4], F32, kind="ExternalInput").ap()
    GW = nc.dram_tensor("GW", [4, D, H1], F32, kind="ExternalInput").ap()
    UW = nc.dram_tensor("UW", [4, D, H1], F32, kind="ExternalInput").ap()
    DW = nc.dram_tensor("DW", [4, H1, D], F32, kind="ExternalInput").ap()
    GB = nc.dram_tensor("GB", [4, 128, 22], F32, kind="ExternalInput").ap()
    UB = nc.dram_tensor("UB", [4, 128, 22], F32, kind="ExternalInput").ap()
    DB = nc.dram_tensor("DB", [4, D], F32, kind="ExternalInput").ap()
    OH4 = nc.dram_tensor("OH4", [4, 512], F32, kind="ExternalInput").ap()
    WINT = nc.dram_tensor("WINT", [D, 64], F32, kind="ExternalInput").ap()
    WOUTT = nc.dram_tensor("WOUTT", [64, D], F32, kind="ExternalInput").ap()
    LOGA = nc.dram_tensor("LOGA", [64, 1], F32, kind="ExternalInput").ap()
    BP = nc.dram_tensor("BP", [64, 1], F32, kind="ExternalInput").ap()
    CP = nc.dram_tensor("CP", [64, 1], F32, kind="ExternalInput").ap()
    LOGDT = nc.dram_tensor("LOGDT", [1, 1], F32, kind="ExternalInput").ap()
    X2OUT = nc.dram_tensor("X2OUT", [D, T], F32, kind="ExternalOutput").ap()
    XN2OUT = nc.dram_tensor("XN2OUT", [D, T], BF16, kind="ExternalOutput").ap()
    S2OUT = nc.dram_tensor("S2OUT", [D, T], BF16, kind="ExternalOutput").ap()

    ar = Arena(P, 204 * 1024)
    mx = MatmulCtx(P, 4)
    cc = CommonCtx(P, mx)
    xn1T = ar.alloc(DC * T, F32, "xn1T")
    xn1b = ar.alloc(DC * T, BF16, "xn1b")
    P.dma("sync", xn1T[:].rearrange("p (c t) -> p c t", c=DC), XN1.rearrange("(c p) t -> p c t", p=128))
    P.dma("gpsimd", xn1b[:].rearrange("p (c t) -> p c t", c=DC), XN1.rearrange("(c p) t -> p c t", p=128))
    P.barrier()
    gbuf = ar.alloc(DC * T, BF16, "gbuf")
    st = {}

    def get_w1f(which, *a):
        if which == "router":
            if "rw" not in st:
                st["rw"] = ar.alloc(DC * 4, F32, "rw")
                P.dma("sync", st["rw"][:].rearrange("p (c n) -> p c n", c=DC),
                      RW1.rearrange("(c p) n -> p c n", p=128))
                P.barrier()
            return st["rw"]
        if which == "onehot4":
            if "oh4" not in st:
                st["oh4"] = ar.alloc(512, F32, "oh4")
                P.dma("sync", st["oh4"][0:4, :], OH4)
                P.barrier()
            return st["oh4"]
        if which == "upscratch":
            if "ups" not in st:
                st["ups"] = ar.alloc(512, BF16, "ups")
            return st["ups"]
        if which == "wbbf":
            if "wbbf" not in st:
                st["wbbf"] = ar.alloc(512, BF16, "wbbf")
            return st["wbbf"]
        if which == "gateup":
            e, hc = a
            h0, hn = H1C[hc]
            if "gw" not in st:
                st["gw"] = ar.alloc(DC * 128, BF16, "gw")
                st["uw"] = ar.alloc(DC * 128, BF16, "uw")
                st["gb"] = ar.alloc(22, F32, "gb")
                st["ub"] = ar.alloc(22, F32, "ub")
                st["gb_e"] = -1
            P.barrier()
            P.dma("gpsimd", st["gw"][:].rearrange("p (k h) -> p k h", k=DC)[:, :, 0:hn],
                  GW[e].rearrange("(k p) h -> p k h", p=128)[:, :, h0:h0 + hn])
            P.dma("gpsimd", st["uw"][:].rearrange("p (k h) -> p k h", k=DC)[:, :, 0:hn],
                  UW[e].rearrange("(k p) h -> p k h", p=128)[:, :, h0:h0 + hn])
            if st["gb_e"] != e:
                P.dma("sync", st["gb"][:, :], GB[e])
                P.dma("sync", st["ub"][:, :], UB[e])
                st["gb_e"] = e
            P.barrier()
            return (st["gw"], st["uw"], st["gb"][:, hc:hc + 1], st["ub"][:, hc:hc + 1])
        if which == "down":
            e, dc = a
            if "dw" not in st:
                st["dw"] = ar.alloc(22 * 128, BF16, "dw")
                st["dbr"] = ar.alloc(128, BF16, "dbr")
            P.barrier()
            P.dma("gpsimd", st["dw"][:].rearrange("p (k c) -> p k c", k=22)[:, 0:21, :],
                  DW[e, 0:2688].rearrange("(k p) c -> p k c", p=128)[:, :, dc * 128:(dc + 1) * 128])
            P.dma("gpsimd", st["dw"][0:42, :].rearrange("p (k c) -> p k c", k=22)[:, 21:22, :],
                  DW[e, 2688:2730].rearrange("q c -> q () c")[:, :, dc * 128:(dc + 1) * 128])
            P.dma("gpsimd", st["dbr"][0:1, :], DB[e:e + 1, dc * 128:(dc + 1) * 128])
            P.barrier()
            return (st["dw"], st["dbr"])
        if which == "free_e":
            return None

    emit_l1_ffn(P, cc, ar, xn1T, xn1b, gbuf, get_w1f, mx, tbig=1024, free_xnT=True)
    for k in ("rw", "oh4", "ups", "wbbf", "gw", "uw", "gb", "ub", "dw", "dbr"):
        if k in st and not isinstance(st[k], int):
            ar.free_(st[k])
    diagall = ar.alloc(T, F32, "diagall")
    P.dma("sync", diagall[:, :], DIAGA)
    w1a = {}
    for nm, dr in (("WvTbd", WVBD),):
        w1a[nm] = ar.alloc(128, BF16, nm)
        P.dma("gpsimd", w1a[nm][:, :], dr)
    w1a["csel"] = ar.alloc(16, BF16, "csel")
    P.dma("gpsimd", w1a["csel"][:, :], CSEL)
    w1a["oneh16"] = ar.alloc(16 * 128, BF16, "oneh16")
    P.dma("gpsimd", w1a["oneh16"][0:16, :], ONEH)
    P.barrier()
    sbuf_ = ar.alloc(DC * T, BF16, "sbuf")
    emit_l1_rest(P, cc, ar, xn1b, diagall, sbuf_, w1a, mx)
    ar.free_(diagall)
    for v in w1a.values():
        ar.free_(v)
    xbuf = ar.alloc(DC * T, F32, "xbuf")
    P.dma("sync", xbuf[:].rearrange("p (c t) -> p c t", c=DC), X1.rearrange("(c p) t -> p c t", p=128))
    P.barrier()
    emit_combine(P, cc, ar, xbuf, sbuf_, gbuf)
    P.dma("sync", X2OUT.rearrange("(c p) t -> p c t", p=128), xbuf[:].rearrange("p (c t) -> p c t", c=DC))
    ar.free_(sbuf_); ar.free_(gbuf); ar.free_(xn1b)
    xn2b = ar.alloc(DC * T, BF16, "xn2b")
    scr = ln_scratch_ar(ar)
    emit_ln(P, cc, xbuf, xbuf, xn2b, scr)
    ln_scratch_free(ar, scr)
    ar.free_(xbuf)
    P.dma("sync", XN2OUT.rearrange("(c p) t -> p c t", p=128), xn2b[:].rearrange("p (c t) -> p c t", c=DC))
    s2 = ar.alloc(DC * T, BF16, "s2")
    l2bufs = {
        "u": ar.alloc(T, F32, "l2a_u")[0:64, :],
        "hsc": ar.alloc(T, F32, "l2a_h")[0:64, :],
        "hbf": ar.alloc(T, BF16, "l2a_hbf")[0:64, :],
        "atile": ar.alloc(T, F32, "l2a_at")[0:64, :],
        "WoutC": ar.alloc(1024, BF16, "l2a_wc")[0:64, :],
    }
    w2a = {
        "WinT": ar.alloc(DC * 64, BF16, "w_wint"),
        "WoutT": ar.alloc(1024, F32, "w_woutt")[0:64, :],
        "logA": P.sbuf("w_loga", [64, 1], F32),
        "Bp": P.sbuf("w_bp", [64, 1], F32),
        "Cp": P.sbuf("w_cp", [64, 1], F32),
        "logdt": P.sbuf("w_logdt", [1, 1], F32),
    }
    P.dma("gpsimd", w2a["WinT"][:].rearrange("p (c n) -> p c n", c=DC),
          WINT.rearrange("(c p) n -> p c n", p=128))
    P.dma("sync", w2a["WoutT"][:], WOUTT)
    P.dma("sync", w2a["logA"][:], LOGA)
    P.dma("sync", w2a["Bp"][:], BP)
    P.dma("sync", w2a["Cp"][:], CP)
    P.dma("sync", w2a["logdt"][:], LOGDT)
    P.barrier()
    emit_l2_attn(P, xn2b, s2, w2a, mx, bufs=l2bufs)
    P.dma("sync", S2OUT.rearrange("(c p) t -> p c t", p=128), s2[:].rearrange("p (c t) -> p c t", c=DC))
    P.barrier()
    return P.finalize()


def build_launch4b():
    P = Prog("launch4b")
    nc = P.nc
    X2 = nc.dram_tensor("X2", [D, T], F32, kind="ExternalInput").ap()
    XN2 = nc.dram_tensor("XN2", [D, T], BF16, kind="ExternalInput").ap()
    S2 = nc.dram_tensor("S2", [D, T], BF16, kind="ExternalInput").ap()
    WQKV = nc.dram_tensor("WQKV", [L2D, D, 3072], F32, kind="ExternalInput").ap()
    QSR = nc.dram_tensor("QSR", [L2D, 128, 1], F32, kind="ExternalInput").ap()
    KSR = nc.dram_tensor("KSR", [L2D, 128, 1], F32, kind="ExternalInput").ap()
    WOg = nc.dram_tensor("WOg", [L2D, 4, 64, 4 * 1024], F32, kind="ExternalInput").ap()
    LNG = nc.dram_tensor("LNG", [L2D, 128, DC], F32, kind="ExternalInput").ap()
    LNB = nc.dram_tensor("LNB", [L2D, 128, DC], F32, kind="ExternalInput").ap()
    W1L2 = nc.dram_tensor("W1L2", [L2D, D, 5460], F32, kind="ExternalInput").ap()
    B1L2 = nc.dram_tensor("B1L2", [L2D, 2, 128, 22], F32, kind="ExternalInput").ap()
    W2L2 = nc.dram_tensor("W2L2", [L2D, H1B, D], F32, kind="ExternalInput").ap()
    B2L2 = nc.dram_tensor("B2L2", [L2D, 128, DC], F32, kind="ExternalInput").ap()
    CSQ = nc.dram_tensor("CSQ", [128, T], F32, kind="ExternalInput").ap()
    SSQ = nc.dram_tensor("SSQ", [128, T], F32, kind="ExternalInput").ap()
    CSK = nc.dram_tensor("CSK", [128, T], F32, kind="ExternalInput").ap()
    SSK = nc.dram_tensor("SSK", [128, T], F32, kind="ExternalInput").ap()
    PROT = nc.dram_tensor("PROT", [128, 128], F32, kind="ExternalInput").ap()
    H2C = nc.dram_tensor("H2C", [128, 2], F32, kind="ExternalInput").ap()
    H2R = nc.dram_tensor("H2R", [2, 128], F32, kind="ExternalInput").ap()
    AMASK = nc.dram_tensor("AMASK", [128, 4 * 256], F32, kind="ExternalInput").ap()
    WQT3 = nc.dram_tensor("WQT3", [D, D], F32, kind="ExternalInput").ap()
    WKVT3 = nc.dram_tensor("WKVT3", [D, 2 * D], F32, kind="ExternalInput").ap()
    WOTH3 = nc.dram_tensor("WOTH3", [64, 16 * 1024], F32, kind="ExternalInput").ap()
    UPL = nc.dram_tensor("UPL", [7, 32, 128, 128], F32, kind="ExternalInput").ap()
    UPH = nc.dram_tensor("UPH", [5, 32, 128, 128], F32, kind="ExternalInput").ap()
    DNL = nc.dram_tensor("DNL", [7, 32, 128, 128], F32, kind="ExternalInput").ap()
    DNH = nc.dram_tensor("DNH", [5, 32, 128, 128], F32, kind="ExternalInput").ap()
    WEXP = nc.dram_tensor("WEXP", [D, 4096], F32, kind="ExternalInput").ap()
    WCON = nc.dram_tensor("WCON", [4096, D], F32, kind="ExternalInput").ap()
    BEXP = nc.dram_tensor("BEXP", [128, 32], F32, kind="ExternalInput").ap()
    BUP = nc.dram_tensor("BUP", [128, 32], F32, kind="ExternalInput").ap()
    BCON = nc.dram_tensor("BCON", [128, 8], F32, kind="ExternalInput").ap()
    DBIA = nc.dram_tensor("DBIA", [128, 8], F32, kind="ExternalInput").ap()
    OUT = nc.dram_tensor("OUT", [D, T], F32, kind="ExternalOutput").ap()
    mqd = nc.dram_tensor("l4b_mqd", [16, 128], F32, kind="Internal").ap()
    x3dram = nc.dram_tensor("l4b_x3", [D, T], F32, kind="Internal").ap()
    pdram = nc.dram_tensor("l4b_pdram", [128, 16384], BF16, kind="Internal").ap()

    ar = Arena(P, 204 * 1024)
    mx = MatmulCtx(P, 4)
    cc = CommonCtx(P, mx)
    xn2b = ar.alloc(DC * T, BF16, "xn2b")
    P.dma("sync", xn2b[:].rearrange("p (c t) -> p c t", c=DC), XN2.rearrange("(c p) t -> p c t", p=128))
    P.barrier()
    st = {}

    def ldc(nm, dr, rows, cols, dt=BF16, q="gpsimd"):
        if nm not in st:
            st[nm] = ar.alloc(cols, dt, nm)
            P.dma(q, st[nm][0:rows, :], dr)
            P.barrier()
        return st[nm]

    def get_w2f(which, *a):
        if which == "rotc":
            return {"csq": ldc("csq_c", CSQ, 128, T), "ssq": ldc("ssq_c", SSQ, 128, T),
                    "csk": ldc("csk_c", CSK, 128, T), "ssk": ldc("ssk_c", SSK, 128, T)}
        if which == "prot":
            return ldc("prot", PROT, 128, 128)
        if which == "h2col":
            return ldc("h2col", H2C, 128, 2, dt=BF16)
        if which == "h2row":
            return ldc("h2row", H2R, 2, 128, dt=F32, q="sync")
        if which == "amask":
            return ldc("amask", AMASK, 128, 4 * 256)
        if which == "mqdram":
            return mqd
        if which == "scales":
            li = a[0]
            if "qs" not in st:
                st["qs"] = ar.alloc(1, F32, "qs"); st["ks"] = ar.alloc(1, F32, "ks")
            P.barrier()
            P.dma("sync", st["qs"][:, :], QSR[li])
            P.dma("sync", st["ks"][:, :], KSR[li])
            P.barrier()
            return (st["qs"][:, 0:1], st["ks"][:, 0:1])
        if which == "qkv":
            li, pair = a
            if "wq" not in st:
                st["wq"] = ar.alloc(DC * 128, BF16, "wq")
                st["wk"] = ar.alloc(DC * 128, BF16, "wk")
                st["wv"] = ar.alloc(DC * 128, BF16, "wv")
            P.barrier()
            for nm, off in (("wq", 0), ("wk", 1024), ("wv", 2048)):
                P.dma("gpsimd", st[nm][:].rearrange("p (k c) -> p k c", k=DC),
                      WQKV[li].rearrange("(k p) n -> p k n", p=128)[:, :, off + pair * 128: off + pair * 128 + 128])
            P.barrier()
            return (st["wq"], st["wk"], st["wv"])
        if which == "wo":
            li, grp = a
            st["wo"] = ar.alloc(4 * 1024, BF16, "wo")
            P.dma("gpsimd", st["wo"][0:64, :], WOg[li, grp])
            P.barrier()
            return st["wo"]
        if which == "free_wo":
            ar.free_(st.pop("wo"))
            return None
        if which == "ln":
            li = a[0]
            if "lng" not in st:
                st["lng"] = ar.alloc(DC, F32, "lng"); st["lnb"] = ar.alloc(DC, F32, "lnb")
            P.barrier()
            P.dma("sync", st["lng"][:, :], LNG[li])
            P.dma("sync", st["lnb"][:, :], LNB[li])
            P.barrier()
            return (st["lng"], st["lnb"])
        if which == "w1":
            li, hc = a
            h0, hn = L2FC[hc]
            if "w1a" not in st:
                st["w1a"] = ar.alloc(DC * 128, BF16, "w1a")
                st["w1g"] = ar.alloc(DC * 128, BF16, "w1g")
                st["b1a"] = ar.alloc(22, F32, "b1a")
                st["b1g"] = ar.alloc(22, F32, "b1g")
                st["b1li"] = -1
            P.barrier()
            P.dma("gpsimd", st["w1a"][:].rearrange("p (k c) -> p k c", k=DC)[:, :, 0:hn],
                  W1L2[li].rearrange("(k p) n -> p k n", p=128)[:, :, h0:h0 + hn])
            P.dma("gpsimd", st["w1g"][:].rearrange("p (k c) -> p k c", k=DC)[:, :, 0:hn],
                  W1L2[li].rearrange("(k p) n -> p k n", p=128)[:, :, 2730 + h0:2730 + h0 + hn])
            if st["b1li"] != li:
                P.dma("sync", st["b1a"][:, :], B1L2[li, 0])
                P.dma("sync", st["b1g"][:, :], B1L2[li, 1])
                st["b1li"] = li
            P.barrier()
            return (st["w1a"], st["w1g"], st["b1a"][:, hc:hc + 1], st["b1g"][:, hc:hc + 1])
        if which == "w2":
            li, dc = a
            if "w2" not in st:
                st["w2"] = ar.alloc(22 * 128, BF16, "w2")
                st["b2"] = ar.alloc(DC, F32, "b2")
                st["b2li"] = -1
            P.barrier()
            P.dma("gpsimd", st["w2"][:].rearrange("p (k c) -> p k c", k=22)[:, 0:21, :],
                  W2L2[li, 0:2688].rearrange("(k p) c -> p k c", p=128)[:, :, dc * 128:(dc + 1) * 128])
            P.dma("gpsimd", st["w2"][0:42, :].rearrange("p (k c) -> p k c", k=22)[:, 21:22, :],
                  W2L2[li, 2688:2730].rearrange("q c -> q () c")[:, :, dc * 128:(dc + 1) * 128])
            if st["b2li"] != li:
                P.dma("sync", st["b2"][:, :], B2L2[li])
                st["b2li"] = li
            P.barrier()
            return (st["w2"], st["b2"][:, dc:dc + 1])
        raise KeyError(which)

    def finish_l2(x2h):
        xt = ar.alloc(T, F32, "fin_xt")
        s2t = ar.alloc(T, BF16, "fin_s2t")
        for dc in range(DC):
            sl = slice(dc * T, (dc + 1) * T)
            P.dma("sync", xt[:, :], X2.rearrange("(c p) t -> p c t", p=128)[:, dc, :])
            P.dma("scalar", s2t[:, :], S2.rearrange("(c p) t -> p c t", p=128)[:, dc, :])
            act(P, x2h[:, sl], x2h[:, sl], AF.Silu)
            P.barrier()
            t_tensor(P, "vector", x2h[:, sl], x2h[:, sl], s2t[:, :], ALU.mult)
            P.barrier()
            t_tensor(P, "vector", xt[:, :], xt[:, :], x2h[:, sl], ALU.add)
            P.barrier()
            P.dma("sync", x3dram.rearrange("(c p) t -> p c t", p=128)[:, dc, :], xt[:, :])
            P.barrier()
        ar.free_(xt); ar.free_(s2t)

    emit_l2_ffn(P, cc, ar, xn2b, finish_l2, get_w2f, mx, free_xn=True)
    for k in ("csq_c", "ssq_c", "csk_c", "ssk_c", "prot", "h2col", "h2row", "amask",
              "qs", "ks", "wq", "wk", "wv", "lng", "lnb", "w1a", "w1g", "b1a", "b1g",
              "w2", "b2"):
        if k in st and not isinstance(st[k], int):
            ar.free_(st[k])
    xb = ar.alloc(DC * T, F32, "xb3")
    P.dma("sync", xb[:].rearrange("p (c t) -> p c t", c=DC), x3dram.rearrange("(c p) t -> p c t", p=128))
    P.barrier()
    xn3b = ar.alloc(DC * T, BF16, "xn3b")
    scr = ln_scratch_ar(ar)
    emit_ln(P, cc, xb, xb, xn3b, scr)
    ln_scratch_free(ar, scr)
    ar.free_(xb)
    s3 = ar.alloc(DC * T, BF16, "s3")
    st3 = {}

    def get_w3(phase, half=None):
        if phase == "qkv":
            st3["WqTh"] = ar.alloc(DC * 512, BF16, "wqth3")
            st3["WkTh"] = ar.alloc(DC * 512, BF16, "wkth3")
            st3["WvTh"] = ar.alloc(DC * 512, BF16, "wvth3")
            P.dma("gpsimd", st3["WqTh"][:].rearrange("p (c n) -> p c n", c=DC),
                  WQT3.rearrange("(c p) n -> p c n", p=128)[:, :, half * 512:half * 512 + 512])
            P.dma("gpsimd", st3["WkTh"][:].rearrange("p (c n) -> p c n", c=DC),
                  WKVT3.rearrange("(c p) n -> p c n", p=128)[:, :, half * 512:half * 512 + 512])
            P.dma("gpsimd", st3["WvTh"][:].rearrange("p (c n) -> p c n", c=DC),
                  WKVT3.rearrange("(c p) n -> p c n", p=128)[:, :, 1024 + half * 512:1024 + half * 512 + 512])
            P.barrier()
            return st3
        if phase == "free_qkv":
            ar.free_(st3["WqTh"]); ar.free_(st3["WkTh"]); ar.free_(st3["WvTh"])
            return None
        if phase == "wo":
            st3["woth"] = ar.alloc(8 * 1024, BF16, "woth3")
            P.dma("gpsimd", st3["woth"][0:64, :], WOTH3[:, half * 8192:half * 8192 + 8192])
            P.barrier()
            return st3["woth"]
        if phase == "free_wo":
            ar.free_(st3["woth"])
            return None

    emit_l3_attn(P, cc, ar, xn3b, s3, get_w3, mx)
    bexp = ar.alloc(32, F32, "bexp")
    bup = ar.alloc(32, F32, "bup")
    bout = ar.alloc(8, F32, "bout")
    dbia = ar.alloc(8, F32, "dbia")
    P.dma("sync", bexp[:, :], BEXP)
    P.dma("sync", bup[:, :], BUP)
    P.dma("sync", bout[:, :], BCON)
    P.dma("sync", dbia[:, :], DBIA)
    P.barrier()
    t_tensor(P, "vector", bout[:, :], bout[:, :], dbia[:, :], ALU.add)
    P.barrier()
    ar.free_(dbia)
    xt3 = ar.alloc(512, F32, "xt3")
    gt3 = ar.alloc(512, BF16, "gt3")
    s_fc = P.sem("fin_comb")

    def get_wbf(which):
        return {"bfmats": (UPL, UPH, DNL, DNH), "wexp": WEXP, "wcon": WCON,
                "bexp": bexp, "bup": bup, "bout": bout, "pdram": pdram}[which]

    def combine_cb(e, dc, tb, pap):
        return e.tensor_scalar(gt3[:, 0:512], pap, bout[:, dc:dc + 1], None, ALU.add)

    def combine_post(dc, tb):
        sl = slice(dc * T + tb * 512, dc * T + tb * 512 + 512)
        P.barrier()
        act(P, gt3[:, 0:512], gt3[:, 0:512], AF.Silu)
        P.dma("sync", xt3[:, 0:512],
              bass.AP(x3dram.tensor, dc * 128 * T + tb * 512, [[T, 128], [1, 512]]))
        P.barrier()
        t_tensor(P, "vector", gt3[:, 0:512], gt3[:, 0:512], s3[:, sl], ALU.mult)
        P.barrier()
        t_tensor(P, "vector", xt3[:, 0:512], xt3[:, 0:512], gt3[:, 0:512], ALU.add)
        P.barrier()
        P.dma("sync", OUT.rearrange("(c p) t -> p c t", p=128)[:, dc, tb * 512:(tb + 1) * 512],
              xt3[:, 0:512])
        P.barrier()

    P._l3f_post = combine_post
    emit_l3_ffn(P, cc, ar, xn3b, get_wbf, mx, combine_cb)
    P.barrier()
    return P.finalize()



def _bd(W):
    Z = np.zeros((128, 128), np.float32)
    Z[0:64, 0:64] = W.T
    Z[64:128, 64:128] = W.T
    return Z


def _l2ffn_consts():
    DIM_HEAD, WINDOW, n = 64, 256, T
    half = np.arange(0, DIM_HEAD, 2, dtype=np.float64)
    inv_freq = 1.0 / 10000.0 ** (half / DIM_HEAD)
    t_ = np.arange(n, dtype=np.float64)
    freqs = np.concatenate([t_[:, None] * inv_freq[None, :]] * 2, -1)
    base_scale = (half + 0.4 * DIM_HEAD) / (1.4 * DIM_HEAD)
    power = (t_ - n // 2) / (WINDOW // 2)
    scale = np.concatenate([base_scale[None, :] ** power[:, None]] * 2, -1)
    cos, sin = np.cos(freqs), np.sin(freqs)
    mk2 = lambda M: np.ascontiguousarray(np.tile(M.T.astype(np.float32), (2, 1)))
    CSQ, SSQ = mk2(cos * scale), mk2(sin * scale)
    CSK, SSK = mk2(cos / scale), mk2(sin / scale)
    Pm = np.zeros((64, 64), np.float32)
    for i2 in range(32):
        Pm[i2, i2 + 32] = -1.0
        Pm[i2 + 32, i2] = 1.0
    PROT = np.zeros((128, 128), np.float32)
    PROT[0:64, 0:64] = Pm.T
    PROT[64:128, 64:128] = Pm.T
    H2C = np.zeros((128, 2), np.float32); H2C[0:64, 0] = 1; H2C[64:128, 1] = 1
    H2R = np.zeros((2, 128), np.float32); H2R[0, 0:64] = 1; H2R[1, 64:128] = 1
    AM = np.zeros((128, 4 * 256), np.float32)
    qr = np.arange(256)
    for stt in range(4):
        for p in range(128):
            krel = stt * 128 - 256 + p
            AM[p, stt * 256:(stt + 1) * 256] = ((qr >= krel) & (qr <= krel + 256)).astype(np.float32)
    return {"CSQ": CSQ, "SSQ": SSQ, "CSK": CSK, "SSK": SSK, "PROT": PROT,
            "H2C": H2C, "H2R": H2R, "AMASK": AM}


def kernel(**inp):
    f32 = np.float32
    g = lambda k: np.ascontiguousarray(np.asarray(inp[k], f32))
    cores = list(range(N_CORES))
    x0 = g("L0_attn_hidden_states")

    wcth = np.ascontiguousarray(g("L0_attn_Wc").T.reshape(16, 64, 1024)
                                .transpose(1, 0, 2).reshape(64, 16 * 1024))
    dexp = np.ascontiguousarray(np.tile(g("L0_attn_index_decays").T, (2, 1)))
    mdrep = np.ascontiguousarray(np.tile(g("L0_attn_mega_decays")[None, :], (32, 1)).astype(f32))
    jj, rr = np.meshgrid(np.arange(32), np.arange(32), indexing="ij")
    ematT = (rr - 1 - jj).astype(f32)
    maskT = (jj <= rr - 1).astype(f32)
    shared1 = {"WQT": np.ascontiguousarray(g("L0_attn_Wq").T),
               "WKT": np.ascontiguousarray(g("L0_attn_Wk").T),
               "WVT": np.ascontiguousarray(g("L0_attn_Wv").T),
               "WCTH": wcth, "DEXP": dexp, "MDREP": mdrep,
               "EMATT": ematT, "MASKT": maskT,
               "TMROW": np.ascontiguousarray(g("L0_attn_terrace_mix").reshape(1, 64))}
    nc1 = build_launch1()
    maps1 = [dict(shared1, X=np.ascontiguousarray(x0[c].T)) for c in cores]
    r1 = run_bass_kernel_spmd(nc1, maps1, core_ids=cores)
    XN0 = [np.asarray(r1.results[c]["XNOUT"]) for c in cores]
    S0 = [np.asarray(r1.results[c]["SOUT"]) for c in cores]

    rw = g("L0_ffn_router_w")
    rwg = np.zeros((16, D, 128), f32)
    for gg in range(16):
        for e in range(8):
            rwg[gg, :, e * 16 + gg] = rw[:, e]
    esel = np.zeros((128, 16), f32); gsel = np.zeros((128, 16), f32)
    gexp = np.zeros((16, 128), f32); eexp = np.zeros((16, 128), f32)
    for p in range(128):
        esel[p, p // 16] = 1; gsel[p, p % 16] = 1
        gexp[p % 16, p] = 1; eexp[p // 16, p] = 1
    oh128 = np.zeros((128, 16 * 128), f32)
    for e in range(8):
        for gg in range(2):
            oh128[e * 16 + gg, (e * 2 + gg) * 128:(e * 2 + gg + 1) * 128] = 1
    rb = g("L0_ffn_router_b")
    rbcol = np.ascontiguousarray(rb[np.arange(128) // 16].reshape(128, 1))
    shared2 = {"RWG": rwg, "ESEL": esel, "GSEL": gsel, "GEXP": gexp, "EEXP": eexp,
               "OH128": oh128, "W1": g("L0_ffn_w1"),
               "B1": np.ascontiguousarray(g("L0_ffn_b1").reshape(8, 32, 128).transpose(0, 2, 1)),
               "W2": g("L0_ffn_w2"), "B2": np.ascontiguousarray(g("L0_ffn_b2").reshape(8, 1, D)),
               "RB": rbcol}
    nc2 = build_launch2()
    maps2 = [dict(shared2, XNF=np.ascontiguousarray(
        np.concatenate([XN0[(c + i) % 8] for i in range(8)], axis=1))) for c in cores]
    r2 = run_bass_kernel_spmd(nc2, maps2, core_ids=cores)
    G0 = [np.asarray(r2.results[c]["G0"]) for c in cores]

    hsel = np.zeros((128, 128), f32)
    for ph in range(8):
        hsel[0:64, ph * 16 + 2 * ph] = 1
        hsel[64:128, ph * 16 + 2 * ph + 1] = 1
    shared3 = {"WQBD": _bd(g("L1_attn_Wq")), "WKBD": _bd(g("L1_attn_Wk")), "HSEL": hsel}
    nc3 = build_launch3()
    maps3 = [dict(shared3, X=np.ascontiguousarray(x0[c].T), S0=S0[c], G0=G0[c]) for c in cores]
    r3 = run_bass_kernel_spmd(nc3, maps3, core_ids=cores)
    X1 = [np.asarray(r3.results[c]["X1OUT"]) for c in cores]
    XN1 = [np.asarray(r3.results[c]["XN1OUT"]) for c in cores]
    DIAG = [np.asarray(r3.results[c]["DIAG"]) for c in cores]

    csel = np.zeros((128, 16), f32)
    for c in range(8):
        for h in range(16):
            csel[c * 16 + h, h] = 1
    oneh = np.zeros((16, 16 * 128), f32)
    for h in range(16):
        oneh[h, h * 128:(h + 1) * 128] = 1
    def padbias_l1(b):
        out = np.zeros((4, 128, 22), f32)
        for hc, (h0, hn) in enumerate(H1C):
            out[:, 0:hn, hc] = b[:, h0:h0 + hn]
        return out
    oh4 = np.zeros((4, 512), f32)
    for e in range(4):
        oh4[e, e * 128:(e + 1) * 128] = 1
    shared4a = {"WVBD": _bd(g("L1_attn_Wv")), "CSEL": csel, "ONEH": oneh,
                "RW1": g("L1_ffn_router_w"),
                "GW": g("L1_ffn_gate_w"), "UW": g("L1_ffn_up_w"), "DW": g("L1_ffn_down_w"),
                "GB": padbias_l1(g("L1_ffn_gate_b")), "UB": padbias_l1(g("L1_ffn_up_b")),
                "DB": g("L1_ffn_down_b"), "OH4": oh4,
                "WINT": np.ascontiguousarray(g("L2_attn_W_in").T),
                "WOUTT": np.ascontiguousarray(g("L2_attn_W_out").T),
                "LOGA": np.ascontiguousarray(g("L2_attn_log_A").reshape(64, 1)),
                "BP": np.ascontiguousarray(g("L2_attn_B_param").reshape(64, 1)),
                "CP": np.ascontiguousarray(g("L2_attn_C_param").reshape(64, 1)),
                "LOGDT": np.ascontiguousarray(g("L2_attn_log_dt").reshape(1, 1))}
    nc4a = build_launch4a()
    maps4a = [dict(shared4a, X1=X1[c], XN1=XN1[c],
                   DIAGA=np.ascontiguousarray(
                       np.concatenate([DIAG[(c + i) % 8] for i in range(8)], axis=0)))
              for c in cores]
    r4a = run_bass_kernel_spmd(nc4a, maps4a, core_ids=cores)
    X2 = [np.asarray(r4a.results[c]["X2OUT"]) for c in cores]
    XN2 = [np.asarray(r4a.results[c]["XN2OUT"]) for c in cores]
    S2 = [np.asarray(r4a.results[c]["S2OUT"]) for c in cores]

    Wo2 = g("L2_ffn_Wo")
    wog = np.zeros((L2D, 4, 64, 4 * 1024), f32)
    for li in range(L2D):
        wr = Wo2[li].reshape(16, 64, 1024)
        for grp in range(4):
            wog[li, grp] = wr[grp * 4:(grp + 1) * 4].transpose(1, 0, 2).reshape(64, 4 * 1024)
    def colchunks6(v):
        return np.ascontiguousarray(v.reshape(6, 8, 128).transpose(0, 2, 1))
    def padb16(b):
        out = np.zeros((6, 2, 128, 22), f32)
        for li in range(6):
            for hc, (h0, hn) in enumerate(L2FC):
                out[li, 0, 0:hn, hc] = b[li, h0:h0 + hn]
                out[li, 1, 0:hn, hc] = b[li, 2730 + h0:2730 + h0 + hn]
        return out
    upl, uph = host_stage_mats(g("L3_ffn_up_weights"))
    dnl, dnh = host_stage_mats(g("L3_ffn_down_weights"))
    perm = bf_perm()
    inv = np.empty(4096, np.int64); inv[perm] = np.arange(4096)
    colchunks = lambda v, n: np.ascontiguousarray(v.reshape(n, 128).T)
    shared4b = dict(_l2ffn_consts())
    shared4b.update({
        "WQKV": g("L2_ffn_Wqkv"),
        "QSR": np.ascontiguousarray(np.tile(g("L2_ffn_q_scale"), (1, 2)).reshape(L2D, 128, 1)),
        "KSR": np.ascontiguousarray(np.tile(g("L2_ffn_k_scale"), (1, 2)).reshape(L2D, 128, 1)),
        "WOg": wog, "LNG": colchunks6(g("L2_ffn_ln_g")), "LNB": colchunks6(g("L2_ffn_ln_b")),
        "W1L2": g("L2_ffn_W1"), "B1L2": padb16(g("L2_ffn_b1")),
        "W2L2": g("L2_ffn_W2"), "B2L2": colchunks6(g("L2_ffn_b2")),
        "WQT3": np.ascontiguousarray(g("L3_attn_Wq").T),
        "WKVT3": np.ascontiguousarray(g("L3_attn_Wkv").T),
        "WOTH3": np.ascontiguousarray(g("L3_attn_Wo").T.reshape(16, 64, 1024)
                                      .transpose(1, 0, 2).reshape(64, 16 * 1024)),
        "UPL": upl, "UPH": uph, "DNL": dnl, "DNH": dnh,
        "WEXP": g("L3_ffn_W_exp"),
        "WCON": np.ascontiguousarray(g("L3_ffn_W_con")[inv]),
        "BEXP": colchunks(g("L3_ffn_b_exp"), 32),
        "BUP": colchunks(np.ascontiguousarray(g("L3_ffn_up_bias")[inv]), 32),
        "BCON": colchunks(g("L3_ffn_b_con"), 8),
        "DBIA": colchunks(g("L3_ffn_down_bias"), 8),
    })
    nc4b = build_launch4b()
    maps4b = [dict(shared4b, X2=X2[c], XN2=XN2[c], S2=S2[c]) for c in cores]
    r4b = run_bass_kernel_spmd(nc4b, maps4b, core_ids=cores)
    out = np.stack([np.asarray(r4b.results[c]["OUT"]).T for c in cores]).astype(f32)
    return out
